# version 27
# baseline (speedup 1.0000x reference)
import contextlib
import numpy as np
import concourse.bass as bass
import concourse.mybir as mybir
from concourse.bass_utils import run_bass_kernel_spmd

F32 = mybir.dt.float32
BF16 = mybir.dt.bfloat16
AF = mybir.ActivationFunctionType
ALU = mybir.AluOpType
AX = mybir.AxisListType

T = 2048
D = 1024
H = 8
E = 8
NT = 16
ND = 8
NCH = 16
EPS_LN = 1e-5
EPS_GLA = 1e-5
SCALE_FFN = float(1.0 / np.sqrt(128.0))
SCALE_GLA = float(128.0 ** -0.5)
NEG_BIG = -1.0e30

N_STAGES = 3


class KB:
    ENGS = ("pe", "dve", "act", "gp", "sp")
    DTAGS = ("dx", "dc", "ds0", "ds1", "dy")
    STEP = {"pe": 1, "dve": 1, "act": 1, "gp": 16, "sp": 16,
            "dx": 16, "dc": 16, "ds0": 16, "ds1": 16, "dy": 16}

    def __init__(self, nc):
        self.nc = nc
        self.cnt = {e: 0 for e in self.ENGS + self.DTAGS}
        self.streams = {e: [] for e in self.ENGS}
        self.last_w = {}
        self.readers = {}
        self.mark = {}
        self.sems = {}

    def op(self, eng, fn, reads=(), writes=(), tag=None):
        pr = [r for r in reads if r and r[0] in ("PSB", "PBB")]
        if pr:
            reads = [r for r in reads if not (r and r[0] in ("PSB", "PBB"))]
            writes = list(writes) + pr
        need = {}
        for r in reads:
            w = self.last_w.get(r)
            if w is not None:
                need[w[0]] = max(need.get(w[0], 0), w[1])
        for r in writes:
            w = self.last_w.get(r)
            if w is not None:
                need[w[0]] = max(need.get(w[0], 0), w[1])
            for e2, c2 in self.readers.get(r, {}).items():
                need[e2] = max(need.get(e2, 0), c2)
        if eng == "pe":
            need.pop("pe", None)
        waits = []
        for tgt in sorted(need):
            c = need[tgt]
            if tgt in self.DTAGS:
                c = self.cnt[tgt]
            key = (eng, tgt)
            if c > self.mark.get(key, 0):
                self.mark[key] = c
                waits.append((tgt, c))
        acct = tag if tag is not None else eng
        self.cnt[acct] += self.STEP[acct]
        my = self.cnt[acct]
        sems = self.sems
        step = self.STEP[acct]

        def emit(eo, waits=tuple(waits), fn=fn, me=acct, st=step):
            for tgt, c in waits:
                eo.wait_ge(sems[tgt], c)
            fn(eo).then_inc(sems[me], st)

        self.streams[eng].append(emit)
        for r in writes:
            self.last_w[r] = (acct, my)
            self.readers[r] = {}
        for r in reads:
            self.readers.setdefault(r, {})[acct] = my
        return (acct, my)

    def final_wait(self, eng, tgt):
        c = self.cnt[tgt]
        sems = self.sems
        self.streams[eng].append(lambda eo, t=tgt, c=c: eo.wait_ge(sems[t], c))


def cdiv(a, b):
    return (a + b - 1) // b


def arnk(c0, c1):
    return [("ARN", b) for b in range(c0 // 512, cdiv(c1, 512))]


def scrk(c0, c1):
    return [("SCR", b) for b in range(c0 // 128, cdiv(c1, 128))]


def sck(c0, c1):
    return [("SC2", b) for b in range(c0 // 128, cdiv(c1, 128))]


def ctxk(c0, c1):
    return [("CTXH", b) for b in range(c0 // 128, cdiv(c1, 128))]


def expk(c0, c1):
    return [("EXPT", b) for b in range(c0 // 128, cdiv(c1, 128))]


def mm(kb, out_ap, lhsT_ap, rhs_ap, start, stop, reads, writes):
    kb.op("pe", lambda eo, o=out_ap, l=lhsT_ap, r=rhs_ap, s=start, p=stop:
          eo.matmul(o, l, r, start=s, stop=p, skip_group_check=True),
          reads=reads, writes=writes)


class Loader:

    def __init__(self, kb, STG):
        self.kb = kb
        self.STG = STG
        self.rr = 0

    def run(self, pieces, q="sp", ceng="gp"):
        for srcf, dstf, w, wkeys in pieces:
            slot = self.rr
            self.rr ^= 1
            stg = self.STG[:, slot * 512: slot * 512 + w]
            skey = ("STG", slot)
            self.kb.op(q, lambda eo, d=stg, s=srcf: eo.dma_start(d, s()),
                       reads=[skey], writes=[skey], tag="ds0" if slot == 0 else "ds1")
            if ceng == "gp":
                self.kb.op("gp", lambda eo, d=dstf, s=stg: eo.tensor_copy(d(), s),
                           reads=[skey], writes=list(wkeys))
            elif ceng == "act":
                self.kb.op(ceng, lambda eo, d=dstf, s=stg: eo.activation(d(), s, AF.Copy),
                           reads=[skey], writes=list(wkeys))
            else:
                self.kb.op(ceng, lambda eo, d=dstf, s=stg: eo.tensor_copy(d(), s),
                           reads=[skey], writes=list(wkeys))


class _unused:
    pass


def w_head_slice_pieces(din_t, dst, dst0, h, wkeys):
    out = []
    for g in range(2):
        def srcf(g=g):
            v = din_t.ap().rearrange("p (c j) -> p c j", c=ND)
            return v[:, g * 4:(g + 1) * 4, h * 128:(h + 1) * 128]

        def dstf(g=g):
            return dst[:, dst0 + g * 512: dst0 + (g + 1) * 512].rearrange(
                "p (c j) -> p c j", c=4)
        out.append((srcf, dstf, 512, wkeys))
    return out


def w_contig_pieces(din_t, col0, ncols, dst, dst0, wkeys):
    out = []
    for i in range(cdiv(ncols, 512)):
        w = min(512, ncols - i * 512)

        def srcf(i=i, w=w):
            return din_t.ap()[:, col0 + i * 512: col0 + i * 512 + w]

        def dstf(i=i, w=w):
            return dst[:, dst0 + i * 512: dst0 + i * 512 + w]
        out.append((srcf, dstf, w, wkeys))
    return out


RXN = [("XN", i) for i in range(NT)]
RSTG = [("STG", 0), ("STG", 1)]
RWA = [("WA", 0), ("WA", 1)]



def emit_ln(kb, B):
    X, XN, ARN, LNS, SCR = B["X"], B["XN"], B["ARN"], B["LNS"], B["SCR"]
    IDT, PBB = B["IDT"], B["PBB"]
    for i in range(NT):
        xt = X[:, i * D:(i + 1) * D]
        rX = ("X", i)
        kb.op("dve", lambda eo, o=SCR[:, 0:6], a=xt[:, 0:512]: eo.bn_stats(o, a),
              reads=[rX], writes=scrk(0, 128))
        kb.op("dve", lambda eo, o=SCR[:, 6:12], a=xt[:, 512:1024]: eo.bn_stats(o, a),
              reads=[rX], writes=scrk(0, 128))
        kb.op("dve", lambda eo, o=LNS[:, 2 * i:2 * i + 2], a=SCR[:, 0:12]: eo.bn_aggr(o, a),
              reads=scrk(0, 128), writes=[("LNS", i)])
    mv = LNS[:, 0:32].rearrange("p (i k) -> p i k", k=2)
    va = LNS[:, 32:48]
    rall = [("LNS", i) for i in range(NT)]
    kb.op("dve", lambda eo, o=va.rearrange("p (i k) -> p i k", k=1), a=mv[:, :, 1:2]:
          eo.tensor_scalar(o, a, EPS_LN, None, ALU.add), reads=rall, writes=[("LNS", "t")])
    kb.op("act", lambda eo, o=va, a=va: eo.activation(o, a, AF.Ln),
          reads=[("LNS", "t")], writes=[("LNS", "t")])
    kb.op("act", lambda eo, o=LNS[:, 48:64], a=va: eo.activation(o, a, AF.Exp, scale=-0.5),
          reads=[("LNS", "t")], writes=[("LNS", "a")])
    kb.op("dve", lambda eo, o=va.rearrange("p (i k) -> p i k", k=1), a=mv[:, :, 0:1]:
          eo.tensor_copy(o, a), reads=rall, writes=[("LNS", "t")])
    kb.op("dve", lambda eo, o=va, a=va, b=LNS[:, 48:64]: eo.tensor_tensor(o, a, b, ALU.mult),
          reads=[("LNS", "t"), ("LNS", "a")], writes=[("LNS", "t")])
    kb.op("dve", lambda eo, o=va, a=va: eo.tensor_scalar(o, a, -1.0, None, ALU.mult),
          reads=[("LNS", "t")], writes=[("LNS", "t")])
    for i in range(NT):
        slot = i % 2
        xst = ARN[:, slot * D:(slot + 1) * D]
        rst = arnk(slot * D, (slot + 1) * D)
        kb.op("act", lambda eo, o=xst, a=X[:, i * D:(i + 1) * D],
              s=LNS[:, 48 + i:49 + i], b=LNS[:, 32 + i:33 + i]:
              eo.activation(o, a, AF.Identity, bias=b, scale=s),
              reads=[("X", i), ("LNS", "a"), ("LNS", "t")], writes=rst)
        pb = PBB[i % 2]
        rpb = ("PBB", i % 2)
        for dc in range(ND):
            kb.op("pe", lambda eo, o=pb[:, dc * 128:(dc + 1) * 128],
                  a=xst[:, dc * 128:(dc + 1) * 128], idt=IDT[:, :]:
                  eo.transpose(o, a, idt),
                  reads=rst + [("IDT",)], writes=[rpb])

        def evac(eo, pbx=pb, i=i):
            src = pbx.ap().rearrange("p (c q) -> p c q", c=ND)
            dstv = B["XN"].ap().rearrange("p (c t) -> p c t", c=ND)[:, :, i * 128:(i + 1) * 128]
            return eo.tensor_copy(dstv, src)
        kb.op("dve", evac, reads=[rpb], writes=[("XN", i)])



def emit_ffn_gates(kb, B, ld, din):
    XN, GTM, GOTM, ARN, SCR, SC2 = (B["XN"], B["GTM"], B["GOTM"], B["ARN"],
                                    B["SCR"], B["SC2"])
    PSB, ROUTE = B["PSB"], B["ROUTE"]
    FSEL0 = 8192
    rFS = arnk(FSEL0, FSEL0 + 1024)
    ld.run(w_contig_pieces(din["fsel_t"], 0, 1024, ARN, FSEL0, rFS), ceng="gp")
    for i in range(NT):
        bank = PSB[i % 2]
        rb = ("PSB", i % 2)
        for dc in range(ND):
            mm(kb, bank[:, 0:128],
               XN[:, dc * T + i * 128: dc * T + (i + 1) * 128],
               ARN[:, FSEL0 + dc * 128:FSEL0 + (dc + 1) * 128], dc == 0, dc == ND - 1,
               reads=[RXN[i]] + rFS, writes=[rb])
        kb.op("act", lambda eo, o=SCR[:, 0:128], a=bank[:, 0:128]:
              eo.activation(o, a, AF.Sigmoid),
              reads=[rb], writes=scrk(0, 128))
        p3 = SCR[:, 0:128].rearrange("p (g e) -> p g e", e=8)
        m1 = SC2[:, 0:16]
        m13 = m1.rearrange("p (g o) -> p g o", o=1)
        m2 = SC2[:, 16:32]
        m23 = m2.rearrange("p (g o) -> p g o", o=1)
        e13 = SCR[:, 128:256].rearrange("p (g e) -> p g e", e=8)
        mk3 = SCR[:, 256:384].rearrange("p (g e) -> p g e", e=8)
        e23 = SCR[:, 384:512].rearrange("p (g e) -> p g e", e=8)
        f = SC2[:, 32:48]
        kb.op("dve", lambda eo, o=m13, a=p3: eo.reduce_max(o, a, AX.X),
              reads=scrk(0, 128), writes=sck(0, 128))
        kb.op("dve", lambda eo, o=e13, a=p3, b=m13.to_broadcast([128, 16, 8]):
              eo.tensor_tensor(o, a, b, ALU.is_equal),
              reads=scrk(0, 128) + sck(0, 128), writes=scrk(128, 256))
        kb.op("dve", lambda eo, o=mk3, a=e13, b=p3:
              eo.scalar_tensor_tensor(o, a, NEG_BIG, b, ALU.mult, ALU.add),
              reads=scrk(0, 256), writes=scrk(256, 384))
        kb.op("dve", lambda eo, o=m23, a=mk3: eo.reduce_max(o, a, AX.X),
              reads=scrk(256, 384), writes=sck(0, 128))
        kb.op("dve", lambda eo, o=f, a=m1, b=m2: eo.tensor_tensor(o, a, b, ALU.add),
              reads=sck(0, 128), writes=sck(0, 128))
        kb.op("dve", lambda eo, o=f, a=f: eo.tensor_scalar(o, a, 1e-9, None, ALU.max),
              reads=sck(0, 128), writes=sck(0, 128))
        kb.op("dve", lambda eo, o=f, a=f: eo.reciprocal(o, a),
              reads=sck(0, 128), writes=sck(0, 128))
        kb.op("dve", lambda eo, o=f, a=f, b=ROUTE[:, 0:1].to_broadcast([128, 16]):
              eo.tensor_tensor(o, a, b, ALU.mult),
              reads=sck(0, 128) + [("ROUTE",)], writes=sck(0, 128))
        kb.op("dve", lambda eo, o=m1, a=m1, b=f: eo.tensor_tensor(o, a, b, ALU.mult),
              reads=sck(0, 128), writes=sck(0, 128))
        kb.op("dve", lambda eo, o=e23, a=mk3, b=m23.to_broadcast([128, 16, 8]):
              eo.tensor_tensor(o, a, b, ALU.is_equal),
              reads=scrk(256, 384) + sck(0, 128), writes=scrk(384, 512))
        kb.op("dve", lambda eo, o=m2, a=m2, b=f: eo.tensor_tensor(o, a, b, ALU.mult),
              reads=sck(0, 128), writes=sck(0, 128))
        a1b = m13.to_broadcast([128, 16, 8])
        a2b = m23.to_broadcast([128, 16, 8])
        kb.op("dve", lambda eo, o=e13, a=e13, b=a1b: eo.tensor_tensor(o, a, b, ALU.mult),
              reads=scrk(128, 256) + sck(0, 128), writes=scrk(128, 256))
        kb.op("dve", lambda eo, o=e23, a=e23, b=a2b: eo.tensor_tensor(o, a, b, ALU.mult),
              reads=scrk(384, 512) + sck(0, 128), writes=scrk(384, 512))
        kb.op("dve", lambda eo, o=GTM[:, i * 64:(i + 1) * 64],
              a=SCR[:, 128:192], b=SCR[:, 384:448]:
              eo.tensor_tensor(o, a, b, ALU.add),
              reads=scrk(128, 512), writes=[("GTM", i)])
        kb.op("dve", lambda eo, o=GOTM[:, i * 64:(i + 1) * 64],
              a=SCR[:, 192:256], b=SCR[:, 448:512]:
              eo.tensor_tensor(o, a, b, ALU.add),
              reads=scrk(128, 512), writes=[("GOTM", i)])
        yield


def emit_ffn_head(kb, B, din, ld, h, last, pending, gates_gen=None):
    XN, GTM, GOTM, GFMH = B["XN"], B["GTM"], B["GOTM"], B["GFMH"]
    QH, KH, VH, WA, WB, QKW = B["QH"], B["KH"], B["VH"], B["WA"], B["WB"], B["QKW"]
    EXPT, CTXH, ARN, SCR, SC2, X = B["EXPT"], B["CTXH"], B["ARN"], B["SCR"], B["SC2"], B["X"]
    PSB, PBB, ONESB, SELE, IDT = B["PSB"], B["PBB"], B["ONESB"], B["SELE"], B["IDT"]

    def emit_gfmh(half):
        pb = PBB[half % 2]
        rpb = ("PBB", half % 2)
        for tl in range(8):
            i = half * 8 + tl
            kb.op("pe", lambda eo, o=pb[0:8, tl * 128:(tl + 1) * 128],
                  a=GOTM[:, i * 64 + h * 8: i * 64 + h * 8 + 8], idt=IDT[:, :]:
                  eo.transpose(o, a, idt),
                  reads=[("GOTM", half * 8 + tl), ("IDT",)], writes=[rpb])
        kb.op("dve", lambda eo, o=GFMH[0:8, 0:1024],
              a=pb[0:8, 0:1024]: eo.tensor_copy(o, a),
              reads=[rpb], writes=[("GFMH",)])
    rgf = [("GFMH",)]

    for wi, dstb, tag in ((0, QH, "QH"), (1, KH, "KH")):
        for g in range(4):
            bank = PSB[g % 2]
            rb = ("PSB", g % 2)
            for dc in range(ND):
                mm(kb, bank[:, :],
                   QKW[:, wi * 1024 + dc * 128: wi * 1024 + (dc + 1) * 128],
                   XN[:, dc * T + g * 512: dc * T + (g + 1) * 512],
                   dc == 0, dc == ND - 1,
                   reads=[RXN[4 * g], RXN[4 * g + 1], RXN[4 * g + 2], RXN[4 * g + 3],
                          ("QKW",)], writes=[rb])
            kb.op("dve", lambda eo, o=dstb[:, g * 512:(g + 1) * 512], a=bank[:, :]:
                  eo.tensor_copy(o, a),
                  reads=[rb], writes=[(tag, g)])
            if gates_gen is not None:
                next(gates_gen, None)
    rqh = [("QH", g) for g in range(4)]
    rkh = [("KH", g) for g in range(4)]

    if not last:
        ld.run(w_head_slice_pieces(din["fq_t"], QKW, 0, h + 1, [("QKW",)])
               + w_head_slice_pieces(din["fk_t"], QKW, 1024, h + 1, [("QKW",)]),
               q="sp", ceng="gp")

    for i in range(NT):
        if gates_gen is not None:
            next(gates_gen, None)
        b0 = 2 if i % 2 == 0 else 4
        banks = (PSB[b0], PSB[b0 + 1])
        rbs = (("PSB", b0), ("PSB", b0 + 1))
        for eg in range(2):
            for dc in range(ND):
                mm(kb, banks[eg][:, :],
                   XN[:, dc * T + i * 128: dc * T + (i + 1) * 128],
                   WA[:, dc * 1024 + eg * 512: dc * 1024 + (eg + 1) * 512],
                   dc == 0, dc == ND - 1,
                   reads=[RXN[i]] + RWA, writes=[rbs[eg]])
        vacc = SCR[:, 0:128]
        for e in range(8):
            bk, rb = (banks[0], rbs[0]) if e < 4 else (banks[1], rbs[1])
            gcol = GTM[:, i * 64 + h * 8 + e: i * 64 + h * 8 + e + 1]
            sl = bk[:, (e % 4) * 128:(e % 4 + 1) * 128]
            if e == 0:
                kb.op("dve", lambda eo, o=vacc, a=sl, g=gcol:
                      eo.tensor_scalar(o, a, g, None, ALU.mult),
                      reads=[rb, ("GTM", i)], writes=scrk(0, 128))
            else:
                kb.op("dve", lambda eo, o=vacc, a=sl, g=gcol, c=vacc:
                      eo.scalar_tensor_tensor(o, a, g, c, ALU.mult, ALU.add),
                      reads=[rb, ("GTM", i)] + scrk(0, 128), writes=scrk(0, 128))
        kb.op("dve", lambda eo, o=VH[:, i * 128:(i + 1) * 128], a=vacc:
              eo.tensor_copy(o, a),
              reads=scrk(0, 128), writes=[("VH", i)])
    rvh = [("VH", i) for i in range(NT)]

    if not last:
        pieces = []
        for dc in range(ND):
            pieces += w_contig_pieces(din["fv"], dc * H * D + (h + 1) * 1024, 1024,
                                      WA, dc * 1024, RWA)
        ld.run(pieces, q="sp", ceng="gp")

    ES = (EXPT[:, 0:512], EXPT[:, 512:1024], CTXH[:, 512:1024])
    rES = (expk(0, 512), expk(512, 1024), ctxk(512, 1024))
    for q4 in range(4):
        t0 = q4 * 512
        if q4 % 2 == 0:
            emit_gfmh(q4 // 2)
        sD, sC = ("PSB", 4), ("PSB", 5)

        def qk_exp(t2):
            a = t2 % 4
            s = t2 % 3
            mm(kb, PSB[a][:, :], KH[:, t2 * 128:(t2 + 1) * 128], QH[:, t0:t0 + 512],
               True, True, reads=[rkh[t2 // 4], rqh[q4]], writes=[("PSB", a)])
            kb.op("act", lambda eo, o=ES[s], i_=PSB[a][:, :]:
                  eo.activation(o, i_, AF.Exp, scale=SCALE_FFN),
                  reads=[("PSB", a)], writes=rES[s])

        def cs_pv(t2):
            s = t2 % 3
            es = ES[s]
            rE = rES[s]
            mm(kb, PSB[5][:, :], VH[:, t2 * 128:(t2 + 1) * 128], es,
               t2 == 0, t2 == NT - 1, reads=rE + [rvh[t2]], writes=[sC])
            mm(kb, PSB[4][:, :], ONESB[:, :], es, t2 == 0, t2 == NT - 1,
               reads=rE + [("ONESB",)], writes=[sD])

        qk_exp(0)
        qk_exp(1)
        for t2 in range(2, NT):
            qk_exp(t2)
            cs_pv(t2 - 2)
        cs_pv(NT - 2)
        cs_pv(NT - 1)
        kb.op("act", lambda eo, o=SC2[:, 0:512], a=PSB[4][:, :]:
              eo.activation(o, a, AF.Ln),
              reads=[sD], writes=sck(0, 512))
        kb.op("act", lambda eo, o=SC2[:, 0:512], a=SC2[:, 0:512]:
              eo.activation(o, a, AF.Exp, scale=-1.0),
              reads=sck(0, 512), writes=sck(0, 512))
        cs = CTXH[:, 0:512]
        rC = ctxk(0, 512)
        kb.op("dve", lambda eo, o=cs, a=PSB[5][:, :], b=SC2[:, 0:512]:
              eo.tensor_tensor(o, a, b, ALU.mult),
              reads=[sC] + sck(0, 512), writes=rC)
        aslot = (q4 % 2) * 4096
        jobs = pending[:]
        del pending[:]
        for e in range(8):
            if e < len(jobs):
                jobs[e]()
            pb = PSB[2 + e % 2]
            rpb = ("PSB", 2 + e % 2)
            gcol0 = t0 % 1024
            mm(kb, pb[:, :], SELE[0:8, e * 128:(e + 1) * 128],
               GFMH[0:8, gcol0:gcol0 + 512], True, True,
               reads=rgf + [("SELE",)], writes=[rpb])
            kb.op("dve", lambda eo, o=ARN[:, aslot + e * 512: aslot + (e + 1) * 512],
                  a=cs, b=pb[:, :]:
                  eo.tensor_tensor(o, a, b, ALU.mult),
                  reads=rC + [rpb], writes=arnk(aslot + e * 512, aslot + (e + 1) * 512))
        for e in range(8, len(jobs)):
            jobs[e]()
        rg = arnk(aslot, aslot + 4096)
        for tl in range(4):
            for dh in range(2):
                def opj(tl=tl, dh=dh, q4=q4, aslot=aslot, rg=rg):
                    ti = q4 * 4 + tl
                    bank = PSB[(tl * 2 + dh) % 2]
                    rb = ("PSB", (tl * 2 + dh) % 2)
                    for e in range(8):
                        mm(kb, bank[:, :],
                           ARN[:, aslot + e * 512 + tl * 128: aslot + e * 512 + (tl + 1) * 128],
                           WB[:, e * 1024 + dh * 512: e * 1024 + (dh + 1) * 512],
                           e == 0, e == E - 1, reads=rg + [("WB",)], writes=[rb])
                    kb.op("dve", lambda eo, o=X[:, ti * D + dh * 512: ti * D + (dh + 1) * 512],
                          a=bank[:, :], x=X[:, ti * D + dh * 512: ti * D + (dh + 1) * 512]:
                          eo.scalar_tensor_tensor(o, a, 0.5, x, ALU.mult, ALU.add),
                          reads=[rb, ("X", ti)], writes=[("X", ti)])
                pending.append(opj)

    for job in pending:
        job()
    del pending[:]

    if not last:
        ld.run(w_contig_pieces(din["fo"], (h + 1) * 8192, 8192, WB, 0, [("WB",)]),
               q="act", ceng="gp")


def emit_ffn(kb, B, din, ld):
    gates_gen = emit_ffn_gates(kb, B, ld, din)
    next(gates_gen, None)
    next(gates_gen, None)
    pieces = []
    for dc in range(ND):
        pieces += w_contig_pieces(din["fv"], dc * H * D, 1024, B["WA"], dc * 1024, RWA)
    ld.run(pieces, q="sp", ceng="gp")
    ld.run(w_head_slice_pieces(din["fq_t"], B["QKW"], 0, 0, [("QKW",)])
           + w_head_slice_pieces(din["fk_t"], B["QKW"], 1024, 0, [("QKW",)]),
           q="sp", ceng="gp")
    ld.run(w_contig_pieces(din["fo"], 0, 8192, B["WB"], 0, [("WB",)]), q="act", ceng="gp")
    pending = []
    for h in range(H):
        emit_ffn_head(kb, B, din, ld, h, last=(h == H - 1), pending=pending,
                      gates_gen=(gates_gen if h == 0 else None))
        if h == 0:
            for _ in gates_gen:
                pass



def emit_gla(kb, B, din, ld):
    XN, QH, VH, U1, WA, WB, QKW = (B["XN"], B["QH"], B["VH"], B["U1"], B["WA"],
                                   B["WB"], B["QKW"])
    ARN, G1, GOTM = B["ARN"], B["KH"], B["GOTM"]
    PSB, PBB, TRIU, IDT, ONESB, ONESF = (B["PSB"], B["PBB"], B["TRIU"], B["IDT"],
                                         B["ONESB"], B["ONESF"])
    X, SCR, SC2, EXPT, CTXH, GNW, STG = (B["X"], B["SCR"], B["SC2"], B["EXPT"],
                                         B["CTXH"], B["GNW"], B["STG"])

    def KT(s):   return SCR[:, s * 128:(s + 1) * 128]
    def rKT(s):  return scrk(s * 128, (s + 1) * 128)
    def DS(s):   return SCR[:, 256 + s * 128:256 + (s + 1) * 128]
    def rDS(s):  return scrk(256 + s * 128, 256 + (s + 1) * 128)
    def FP(s):   return SC2[:, s * 128:(s + 1) * 128]
    def rFP(s):  return sck(s * 128, (s + 1) * 128)
    def SS(s):   return SC2[:, 256 + s * 128:256 + (s + 1) * 128]
    def rSS(s):  return sck(256 + s * 128, 256 + (s + 1) * 128)
    def PTM(s):  return CTXH[:, s * 128:(s + 1) * 128]
    def rPTM(s): return ctxk(s * 128, (s + 1) * 128)
    def POF(s):  return CTXH[:, 256 + s * 128:256 + (s + 1) * 128]
    def rPOF(s): return ctxk(256 + s * 128, 256 + (s + 1) * 128)
    def KTT(s):  return CTXH[:, 512 + s * 128:512 + (s + 1) * 128]
    def rKTT(s): return ctxk(512 + s * 128, 512 + (s + 1) * 128)
    def SPB(s):  return CTXH[:, 768 + s * 128:768 + (s + 1) * 128]
    def rSPB(s): return ctxk(768 + s * 128, 768 + (s + 1) * 128)
    def KTQ(s):  return EXPT[:, s * 512:(s + 1) * 512]
    def rKTQ(s): return expk(s * 512, (s + 1) * 512)
    def CARRY(s): return B["LNS"][:, 2 * s:2 * s + 1]
    def rCARRY(s): return [("LNS", s)]
    def QHB(s):  return QH[:, s * 1024:(s + 1) * 1024]
    def KFB(s):  return U1[:, s * 1024:(s + 1) * 1024]
    def VHB(s):  return VH[:, s * 1024:(s + 1) * 1024]

    ld.run(w_contig_pieces(din["wg1_t"], 0, 1024, QKW, 0, [("QKW",)]), q="sp", ceng="gp")
    ld.run(w_contig_pieces(din["wg2_t"], 0, 1024, QKW, 1024, [("QKW",)]), q="sp", ceng="gp")

    def qkv_pieces(h):
        base = (h % 2) * 4096
        ks = [("WA", h % 2)]
        return (w_head_slice_pieces(din["wq_t"], WA, base, h, ks)
                + w_head_slice_pieces(din["wk_t"], WA, base + 1024, h, ks)
                + w_head_slice_pieces(din["wv_t"], WA, base + 2048, h, ks))
    ld.run(qkv_pieces(0), q="sp", ceng="gp")
    ld.run(qkv_pieces(1), q="sp", ceng="gp")

    for g in range(4):
        bank = PSB[g % 2]
        rb = ("PSB", g % 2)
        for dc in range(ND):
            mm(kb, bank[:, :], QKW[:, dc * 128:(dc + 1) * 128],
               XN[:, dc * T + g * 512: dc * T + (g + 1) * 512], dc == 0, dc == ND - 1,
               reads=[RXN[4 * g], RXN[4 * g + 1], RXN[4 * g + 2], RXN[4 * g + 3],
                      ("QKW",)], writes=[rb])
        kb.op("dve", lambda eo, o=G1[:, g * 512:(g + 1) * 512], a=bank[:, :]:
              eo.tensor_copy(o, a),
              reads=[rb], writes=[("KH", g)])
    rg1 = [("KH", g) for g in range(4)]

    def proj_q(s, h, qtr):
        base = (h % 2) * 4096
        rwa = ("WA", h % 2)
        slot = qtr % 2
        bq = ("QH2", s, slot)
        bk = ("U12", s, slot)
        bv = ("VH2", s, slot)
        t512 = qtr
        rx = [RXN[4 * t512 + j] for j in range(4)]
        bank = PSB[4 + s]
        rb = ("PSB", 4 + s)
        for dc in range(ND):
            mm(kb, bank[:, :],
               WA[:, base + dc * 128: base + (dc + 1) * 128],
               XN[:, dc * T + t512 * 512: dc * T + (t512 + 1) * 512],
               dc == 0, dc == ND - 1, reads=rx + [rwa], writes=[rb])
            yield
        kb.op("act", lambda eo, o=QHB(s)[:, slot * 512:(slot + 1) * 512], a=bank[:, :]:
              eo.activation(o, a, AF.Silu), reads=[rb], writes=[bq])
        yield
        for dc in range(ND):
            mm(kb, bank[:, :],
               WA[:, base + 1024 + dc * 128: base + 1024 + (dc + 1) * 128],
               XN[:, dc * T + t512 * 512: dc * T + (t512 + 1) * 512],
               dc == 0, dc == ND - 1, reads=rx + [rwa], writes=[rb])
            yield
        kb.op("dve", lambda eo, o=KFB(s)[:, slot * 512:(slot + 1) * 512], a=bank[:, :]:
              eo.tensor_copy(o, a), reads=[rb], writes=[bk])
        yield
        for dc in range(ND):
            mm(kb, bank[:, :],
               WA[:, base + 2048 + dc * 128: base + 2048 + (dc + 1) * 128],
               XN[:, dc * T + t512 * 512: dc * T + (t512 + 1) * 512],
               dc == 0, dc == ND - 1, reads=rx + [rwa], writes=[rb])
            yield
        vstg = GOTM[:, s * 512:(s + 1) * 512]
        rvs = ("GOTM", s)
        kb.op("dve", lambda eo, o=vstg, a=bank[:, :]: eo.tensor_copy(o, a),
              reads=[rb], writes=[rvs])
        yield
        pbv = PBB[s]
        rpbv = ("PBB", s)
        for tl in range(4):
            kb.op("pe", lambda eo, o=pbv[:, tl * 128:(tl + 1) * 128],
                  a=vstg[:, tl * 128:(tl + 1) * 128], idt=IDT[:, :]:
                  eo.transpose(o, a, idt),
                  reads=[rvs, ("IDT",)], writes=[rpbv])
            yield
        kb.op("dve", lambda eo, o=VHB(s)[:, slot * 512:(slot + 1) * 512],
              a=pbv[:, 0:512]: eo.tensor_copy(o, a),
              reads=[rpbv], writes=[bv])
        yield
    def fe_gen(s, h, qtr):
        slot = qtr % 2
        bq = ("QH2", s, slot)
        bk = ("U12", s, slot)
        kslot = KFB(s)[:, slot * 512:(slot + 1) * 512]
        kb.op("act", lambda eo, o=kslot, a=kslot: eo.activation(o, a, AF.Exp),
              reads=[bk], writes=[bk])
        yield
        kb.op("dve", lambda eo, o=KTQ(s), a=kslot: eo.tensor_copy(o, a),
              reads=[bk], writes=rKTQ(s))
        yield
        init = 0.0 if qtr == 0 else CARRY(s)
        rinit = [] if qtr == 0 else rCARRY(s)
        kb.op("dve", lambda eo, o=kslot, d0=ONESF[:, 0:1].to_broadcast([128, 512]),
              d1=kslot, i0=init:
              eo.tensor_tensor_scan(o, d0, d1, i0, ALU.mult, ALU.add),
              reads=[bk, ("ONESF",)] + rinit, writes=[bk])
        yield
        kb.op("dve", lambda eo, o=CARRY(s), a=kslot[:, 511:512]: eo.tensor_copy(o, a),
              reads=[bk], writes=rCARRY(s))
        yield
        kb.op("act", lambda eo, o=kslot, a=kslot: eo.activation(o, a, AF.Ln),
              reads=[bk], writes=[bk])
        yield
        kb.op("act", lambda eo, o=kslot, a=kslot: eo.activation(o, a, AF.Exp, scale=-1.0),
              reads=[bk], writes=[bk])
        yield
        qslot = QHB(s)[:, slot * 512:(slot + 1) * 512]
        kb.op("dve", lambda eo, o=qslot, a=qslot, b=kslot:
              eo.tensor_tensor(o, a, b, ALU.mult),
              reads=[bq, bk], writes=[bq])

    def fe_pair(hA, hB, qtr):
        import itertools as _it
        for a, b in _it.zip_longest(fe_gen(0, hA, qtr), fe_gen(1, hB, qtr)):
            pass

    def chunk(s, h, c):
        slot = (c // 4) % 2
        cl = slot * 512 + (c % 4) * 128
        qsl = QHB(s)[:, cl:cl + 128]
        vsl = VHB(s)[:, cl:cl + 128]
        ksl = KTQ(s)[:, (c % 4) * 128:(c % 4 + 1) * 128]
        bq, bv = ("QH2", s, slot), ("VH2", s, slot)
        bPT = ("PSB", 2 * s)
        bO = ("PSB", 2 * s + 1)
        even = (c % 2 == 0)
        last_w = c >= NCH - 2
        pt_dst = PSB[2 * s][:, 0:128] if even else PSB[2 * s][:, 256:384]
        mm(kb, pt_dst, ksl, qsl, even, True,
           reads=rKTQ(s) + [bq], writes=[bPT])
        yield
        if not even:
            ksl_prev = KTQ(s)[:, (c % 4 - 1) * 128:(c % 4) * 128]
            mm(kb, PSB[2 * s][:, 384:512], ksl_prev, qsl, False, True,
               reads=rKTQ(s) + [bq], writes=[bPT])
            yield
        kb.op("dve", lambda eo, o=PTM(s), a=pt_dst, b=TRIU[:, :]:
              eo.tensor_tensor(o, a, b, ALU.mult),
              reads=[bPT, ("TRIU",)], writes=rPTM(s))
        yield
        if not even:
            kb.op("act", lambda eo, o=POF(s), a=PSB[2 * s][:, 384:512]:
                  eo.activation(o, a, AF.Identity),
                  reads=[bPT], writes=rPOF(s))
            yield
        if not last_w:
            kb.op("pe", lambda eo, o=PBB[s][:, 512:640], a=ksl, idt=IDT[:, :]:
                  eo.transpose(o, a, idt),
                  reads=rKTQ(s) + [("IDT",)], writes=[("PBB", s)])
            yield
            kb.op("act", lambda eo, o=KTT(s), a=PBB[s][:, 512:640]:
                  eo.activation(o, a, AF.Identity),
                  reads=[("PBB", s)], writes=rKTT(s))
            yield
        oreg = PSB[2 * s + 1][:, (c % 2) * 128:(c % 2) * 128 + 128]
        mm(kb, oreg, vsl, PTM(s), even, False,
           reads=[bv] + rPTM(s), writes=[bO])
        yield
        if not even:
            vprev = VHB(s)[:, cl - 128:cl]
            mm(kb, oreg, vprev, POF(s), False, False,
               reads=[bv] + rPOF(s), writes=[bO])
            yield
        mm(kb, oreg, SPB(s), qsl, False, True,
           reads=rSPB(s) + [bq], writes=[bO])
        yield
        if not last_w:
            mm(kb, PSB[2 * s][:, 128:256], KTT(s), vsl, False, not even,
               reads=rKTT(s) + [bv], writes=[bPT])
            yield
            if not even:
                kb.op("dve", lambda eo, o=SPB(s), a=SPB(s), b=PSB[2 * s][:, 128:256]:
                      eo.tensor_tensor(o, a, b, ALU.add),
                      reads=rSPB(s) + [bPT], writes=rSPB(s))
                yield
        if not even:
            oc0 = h * T + (c - 1) * 128
            kb.op("act", lambda eo, o=ARN[:, oc0:oc0 + 256], a=PSB[2 * s + 1][:, 0:256]:
                  eo.activation(o, a, AF.Identity, scale=SCALE_GLA),
                  reads=[bO], writes=arnk(oc0, oc0 + 256))

    import itertools
    from collections import deque

    for hp in range(4):
        hA, hB = 2 * hp, 2 * hp + 1
        bg = deque()

        def drain_bg():
            while bg:
                g0 = bg[0]
                try:
                    next(g0)
                except StopIteration:
                    bg.popleft()

        def run_pair(c):
            ga, gb = chunk(0, hA, c), chunk(1, hB, c)
            while True:
                alive = False
                for g0 in (ga, gb):
                    try:
                        next(g0)
                        alive = True
                    except StopIteration:
                        pass
                if bg:
                    try:
                        next(bg[0])
                    except StopIteration:
                        bg.popleft()
                if not alive:
                    break

        ups = []
        if hp == 0:
            ups += [proj_q(0, hA, 0), proj_q(1, hB, 0)]
        ups += [proj_q(0, hA, 1), proj_q(1, hB, 1)]
        for g0 in ups:
            for _ in g0:
                pass
        fe_pair(hA, hB, 0)
        for s, h in ((0, hA), (1, hB)):
            kb.op("dve", lambda eo, o=SPB(s): eo.memset(o, 0.0), reads=[], writes=rSPB(s))
        for c in range(NCH):
            run_pair(c)
            if c == 3:
                fe_pair(hA, hB, 1)
                bg.append(proj_q(0, hA, 2))
                bg.append(proj_q(1, hB, 2))
            if c == 7:
                drain_bg()
                fe_pair(hA, hB, 2)
                bg.append(proj_q(0, hA, 3))
                bg.append(proj_q(1, hB, 3))
            if c == 11:
                drain_bg()
                fe_pair(hA, hB, 3)
                if hp + 1 < 4:
                    ld.run(qkv_pieces(2 * hp + 2), q="sp", ceng="gp")
                    ld.run(qkv_pieces(2 * hp + 3), q="sp", ceng="gp")
                    bg.append(proj_q(0, 2 * hp + 2, 0))
                    bg.append(proj_q(1, 2 * hp + 3, 0))
        drain_bg()
        if hp == 0:
            ld.run(w_contig_pieces(din["wo_t"], 0, 8192, WB, 0, [("WB",)]), q="act", ceng="gp")

    wo_jobs = []

    def wo_job(dc, g):
        bank = PSB[2 + (dc % 2)]
        rb = ("PSB", 2 + (dc % 2))
        for hh in range(ND):
            mm(kb, bank[:, :],
               WB[:, hh * 1024 + dc * 128: hh * 1024 + (dc + 1) * 128],
               ARN[:, hh * T + g * 512: hh * T + (g + 1) * 512],
               hh == 0, hh == ND - 1,
               reads=arnk(hh * T + g * 512, hh * T + (g + 1) * 512) + [("WB",)],
               writes=[rb])
        cslot = dc % 2
        kb.op("dve", lambda eo, o=CTXH[:, cslot * 512:(cslot + 1) * 512],
              a=bank[:, :]: eo.tensor_copy(o, a),
              reads=[rb], writes=ctxk(cslot * 512, (cslot + 1) * 512))
        pb = PBB[dc % 2]
        rpb = ("PBB", dc % 2)
        for tl in range(4):
            kb.op("pe", lambda eo, o=pb[:, tl * 128:(tl + 1) * 128],
                  a=CTXH[:, cslot * 512 + tl * 128: cslot * 512 + (tl + 1) * 128],
                  idt=IDT[:, :]: eo.transpose(o, a, idt),
                  reads=ctxk(cslot * 512, (cslot + 1) * 512) + [("IDT",)],
                  writes=[rpb])

        def xadd(eo, pb=pb, g=g, dc=dc):
            src_ = pb[:, 0:512].rearrange("p (q c) -> p q c", q=4)
            dstv = B["X"].ap().rearrange("p (i d) -> p i d", i=NT)[
                :, g * 4:(g + 1) * 4, dc * 128:(dc + 1) * 128]
            return eo.tensor_tensor(dstv, dstv, src_, ALU.add)
        kb.op("dve", xadd,
              reads=[rpb] + [("X", g * 4 + tl) for tl in range(4)],
              writes=[("X", g * 4 + tl) for tl in range(4)])

    for g in range(4):
        sq = PSB[4 + g % 2]
        rsq = ("PSB", 4 + g % 2)
        jobs = wo_jobs[:]
        del wo_jobs[:]
        ji = 0
        for hh in range(H):
            sqb = (EXPT[:, 0:512] if hh % 2 == 0 else EXPT[:, 512:1024])
            rsqb = (expk(0, 512) if hh % 2 == 0 else expk(512, 1024))
            kb.op("act", lambda eo, o=sqb, a=ARN[:, hh * T + g * 512: hh * T + (g + 1) * 512]:
                  eo.activation(o, a, AF.Square),
                  reads=arnk(hh * T + g * 512, hh * T + (g + 1) * 512),
                  writes=rsqb)
            mm(kb, sq[:, :], ONESB[:, :], sqb, hh == 0, hh == H - 1,
               reads=rsqb + [("ONESB",)], writes=[rsq])
            if ji < len(jobs):
                jobs[ji]()
                ji += 1
        inv = SCR[:, 0:512]
        rinv = scrk(0, 512)
        kb.op("dve", lambda eo, o=inv, a=sq[:, :]:
              eo.tensor_scalar(o, a, 1.0 / D, EPS_GLA, ALU.mult, ALU.add),
              reads=[rsq], writes=rinv)
        kb.op("act", lambda eo, o=inv, a=inv: eo.activation(o, a, AF.Ln),
              reads=rinv, writes=rinv)
        kb.op("act", lambda eo, o=inv, a=inv: eo.activation(o, a, AF.Exp, scale=-0.5),
              reads=rinv, writes=rinv)
        for h in range(H):
            bank = PSB[h % 2]
            rb = ("PSB", h % 2)
            mm(kb, bank[:, :], QKW[:, 1024 + h * 128: 1024 + (h + 1) * 128],
               G1[:, g * 512:(g + 1) * 512], True, True,
               reads=[rg1[g], ("QKW",)], writes=[rb])
            gb = VH[:, 0:512]
            kb.op("act", lambda eo, o=gb, a=bank[:, :]: eo.activation(o, a, AF.Silu),
                  reads=[rb], writes=[("VH2", 0, 0)])
            c0 = h * T + g * 512
            osl = ARN[:, c0:c0 + 512]
            ro = arnk(c0, c0 + 512)
            tmp = SC2[:, 0:512]
            kb.op("dve", lambda eo, o=tmp, a=osl, b=gb:
                  eo.tensor_tensor(o, a, b, ALU.mult),
                  reads=ro + [("VH2", 0, 0)] + sck(0, 512), writes=sck(0, 512))
            kb.op("dve", lambda eo, o=tmp, a=tmp, b=inv:
                  eo.tensor_tensor(o, a, b, ALU.mult),
                  reads=sck(0, 512) + rinv, writes=sck(0, 512))
            kb.op("dve", lambda eo, o=osl, a=tmp, gw=GNW[:, h:h + 1]:
                  eo.tensor_scalar(o, a, gw, None, ALU.mult),
                  reads=sck(0, 512) + [("GNW",)], writes=ro)
            if ji < len(jobs):
                jobs[ji]()
                ji += 1
        for j in range(ji, len(jobs)):
            jobs[j]()
        for dc in range(ND):
            wo_jobs.append(lambda dc=dc, g=g: wo_job(dc, g))
    for j in wo_jobs:
        j()



def build_program():
    nc = bass.Bass("TRN2", target_bir_lowering=False)
    din = {}

    def inp(name, shape, dtype=F32):
        din[name] = nc.dram_tensor(name, list(shape), dtype, kind="ExternalInput")

    inp("x", [128, NT * D])
    inp("wq_t", [128, ND * D]); inp("wk_t", [128, ND * D]); inp("wv_t", [128, ND * D])
    inp("wg1_t", [128, ND * 128]); inp("wg2_t", [128, D]); inp("wo_t", [128, ND * D])
    inp("gnw", [128, ND])
    inp("fq_t", [128, ND * D]); inp("fk_t", [128, ND * D])
    inp("fv", [128, ND * H * D]); inp("fo", [128, H * E * D])
    inp("fsel_t", [128, ND * 128]); inp("route", [128, 1])
    inp("ident", [128, 128], BF16); inp("triu", [128, 128], BF16)
    inp("onesb", [128, 128], BF16); inp("onesf", [128, 128], F32)
    inp("sele", [8, 8 * 128], BF16)
    y = nc.dram_tensor("y", [128, NT * D], F32, kind="ExternalOutput")

    stk = contextlib.ExitStack()

    def sb(name, part, free, dt):
        return stk.enter_context(nc.sbuf_tensor(name, [part, free], dt))

    def psr(name, part, free, dt):
        return stk.enter_context(nc.psum_tensor(name, [part, free], dt))

    with stk:
        B = {}
        B["X"] = sb("X", 128, NT * D, F32)
        B["XN"] = sb("XN", 128, ND * T, BF16)
        B["ARN"] = sb("ARN", 128, 16384, BF16)
        B["WA"] = sb("WA", 128, 8192, BF16)
        B["WB"] = sb("WB", 128, 8192, BF16)
        B["QKW"] = sb("QKW", 128, 2048, BF16)
        B["STG"] = sb("STG", 128, 1024, F32)
        B["QH"] = sb("QH", 128, T, BF16)
        B["KH"] = sb("KH", 128, T, BF16)
        B["VH"] = sb("VH", 128, T, BF16)
        B["U1"] = sb("U1", 128, T, F32)
        B["EXPT"] = sb("EXPT", 128, 1024, BF16)
        B["CTXH"] = sb("CTXH", 128, 1024, BF16)
        B["GTM"] = sb("GTM", 128, NT * 64, F32)
        B["GOTM"] = sb("GOTM", 128, NT * 64, BF16)
        B["GFMH"] = sb("GFMH", 8, 1024, BF16)
        B["SCR"] = sb("SCR", 128, 512, F32)
        B["SC2"] = sb("SC2", 128, 512, F32)
        B["LNS"] = sb("LNS", 128, 64, F32)
        B["IDT"] = sb("IDT", 128, 128, BF16)
        B["TRIU"] = sb("TRIU", 128, 128, BF16)
        B["ONESB"] = sb("ONESB", 128, 128, BF16)
        B["ONESF"] = sb("ONESF", 128, 128, F32)
        B["SELE"] = sb("SELE", 8, 8 * 128, BF16)
        B["ROUTE"] = sb("ROUTE", 128, 1, F32)
        B["GNW"] = sb("GNW", 128, ND, F32)
        PSP = psr("psp", 128, 2048, F32)
        B["PSP"] = PSP
        B["PSB"] = [PSP[:, i * 512:(i + 1) * 512] for i in range(4)] + [
            psr("psb4", 128, 512, F32), psr("psb5", 128, 512, F32)]
        B["PBB"] = [psr(f"pbb{i}", 128, 1024, BF16) for i in range(2)]

        kb = KB(nc)
        with contextlib.ExitStack() as sst:
            for e in KB.ENGS + KB.DTAGS:
                kb.sems[e] = sst.enter_context(nc.semaphore("sem_" + e))
            ld = Loader(kb, B["STG"])

            for i in range(NT):
                kb.op("sp", lambda eo, d=B["X"][:, i * D:(i + 1) * D],
                      s=din["x"].ap()[:, i * D:(i + 1) * D]: eo.dma_start(d, s),
                      reads=[], writes=[("X", i)], tag="dx")
            for nm, bufn in (("ident", "IDT"), ("triu", "TRIU"), ("onesb", "ONESB"),
                             ("onesf", "ONESF"), ("sele", "SELE"),
                             ("route", "ROUTE"), ("gnw", "GNW")):
                kb.op("act", lambda eo, d=B[bufn][:, :], s=din[nm].ap(): eo.dma_start(d, s),
                      reads=[], writes=[(bufn,)], tag="dc")

            emit_ln(kb, B)
            emit_ffn(kb, B, din, ld)
            if N_STAGES >= 2:
                emit_ln(kb, B)
                emit_gla(kb, B, din, ld)
            if N_STAGES >= 3:
                emit_ln(kb, B)
                emit_ffn(kb, B, din, ld)

            for i in range(NT):
                kb.op("act", lambda eo, d=y.ap()[:, i * D:(i + 1) * D],
                      s=B["X"][:, i * D:(i + 1) * D]: eo.dma_start(d, s),
                      reads=[("X", i)], writes=[("Y", i)], tag="dy")
            kb.final_wait("act", "dy")

            with nc.Block() as blk:
                @blk.tensor
                def _(t):
                    for fn in kb.streams["pe"]:
                        fn(t)

                @blk.vector
                def _(v):
                    for fn in kb.streams["dve"]:
                        fn(v)

                @blk.scalar
                def _(s):
                    for fn in kb.streams["act"]:
                        fn(s)

                @blk.gpsimd
                def _(g):
                    for fn in kb.streams["gp"]:
                        fn(g)

                @blk.sync
                def _(s):
                    for fn in kb.streams["sp"]:
                        fn(s)
    return nc



def _prep_host(inputs):
    import ml_dtypes
    f = np.ascontiguousarray
    x = np.asarray(inputs["L0_attn_hidden_states"], dtype=np.float32)

    def fm_w(wt):
        w = np.asarray(wt, dtype=np.float32).T
        din_, dout = w.shape
        return f(w.reshape(din_ // 128, 128, dout).transpose(1, 0, 2).reshape(128, -1))

    sele = np.zeros((8, 8 * 128), dtype=np.float32)
    for e in range(8):
        sele[e, e * 128:(e + 1) * 128] = 1.0

    shared = {
        "wq_t": fm_w(inputs["L0_attn_Wq"]),
        "wk_t": fm_w(inputs["L0_attn_Wk"]),
        "wv_t": fm_w(inputs["L0_attn_Wv"]),
        "wg1_t": fm_w(inputs["L0_attn_Wg1"]),
        "wg2_t": f(np.asarray(inputs["L0_attn_Wg2"], dtype=np.float32).T),
        "wo_t": fm_w(inputs["L0_attn_Wo"]),
        "fq_t": fm_w(inputs["L0_ffn_q_w"]),
        "fk_t": fm_w(inputs["L0_ffn_k_w"]),
        "route": f(np.broadcast_to(
            np.asarray(inputs["L0_ffn_route_scale"], np.float32).reshape(1, 1), (128, 1))),
        "ident": np.eye(128, dtype=np.float32).astype(ml_dtypes.bfloat16),
        "triu": np.triu(np.ones((128, 128), np.float32)).astype(ml_dtypes.bfloat16),
        "onesb": np.ones((128, 128), np.float32).astype(ml_dtypes.bfloat16),
        "onesf": np.ones((128, 128), dtype=np.float32),
        "sele": sele.astype(ml_dtypes.bfloat16),
    }
    gn = np.asarray(inputs["L0_attn_g_norm_weight"], np.float32)
    shared["gnw"] = f(gn.reshape(ND, 128).T)
    vw = np.asarray(inputs["L0_ffn_v_w"], np.float32).reshape(H, E, D, 128)
    fv = vw.transpose(2, 0, 1, 3)
    shared["fv"] = f(fv.reshape(ND, 128, H * E * 128).transpose(1, 0, 2).reshape(128, -1))
    ow = np.asarray(inputs["L0_ffn_o_w"], np.float32).reshape(H, E, 128, D)
    shared["fo"] = f(ow.transpose(2, 0, 1, 3).reshape(128, -1))
    sv = np.asarray(inputs["L0_ffn_sel_v_w"], np.float32)
    so = np.asarray(inputs["L0_ffn_sel_o_w"], np.float32)
    sel = np.concatenate([sv, so], axis=0).T
    shared["fsel_t"] = f(sel.reshape(ND, 128, 128).transpose(1, 0, 2).reshape(128, -1))

    in_maps = []
    for b in range(8):
        m = dict(shared)
        m["x"] = f(x[b].reshape(NT, 128, D).transpose(1, 0, 2).reshape(128, -1))
        in_maps.append(m)
    return in_maps


def kernel(**inputs):
    in_maps = _prep_host(inputs)
    nc = build_program()
    res = run_bass_kernel_spmd(nc, in_maps, core_ids=list(range(8)))
    outs = []
    for r in res.results:
        yb = np.asarray(r["y"])
        outs.append(yb.reshape(128, NT, D).transpose(1, 0, 2).reshape(T, D))
    return np.stack(outs, axis=0)


# version 28 (final)
# speedup vs baseline: 1.0059x; 1.0059x over previous
import contextlib
import numpy as np
import concourse.bass as bass
import concourse.mybir as mybir
from concourse.bass_utils import run_bass_kernel_spmd

F32 = mybir.dt.float32
BF16 = mybir.dt.bfloat16
AF = mybir.ActivationFunctionType
ALU = mybir.AluOpType
AX = mybir.AxisListType

T = 2048
D = 1024
H = 8
E = 8
NT = 16
ND = 8
NCH = 16
EPS_LN = 1e-5
EPS_GLA = 1e-5
SCALE_FFN = float(1.0 / np.sqrt(128.0))
SCALE_GLA = float(128.0 ** -0.5)
NEG_BIG = -1.0e30

N_STAGES = 3


class KB:
    ENGS = ("pe", "dve", "act", "gp", "sp")
    DTAGS = ("dx", "dc", "ds0", "ds1", "dy")
    STEP = {"pe": 1, "dve": 1, "act": 1, "gp": 16, "sp": 16,
            "dx": 16, "dc": 16, "ds0": 16, "ds1": 16, "dy": 16}

    def __init__(self, nc):
        self.nc = nc
        self.cnt = {e: 0 for e in self.ENGS + self.DTAGS}
        self.streams = {e: [] for e in self.ENGS}
        self.last_w = {}
        self.readers = {}
        self.mark = {}
        self.sems = {}

    def op(self, eng, fn, reads=(), writes=(), tag=None):
        pr = [r for r in reads if r and r[0] in ("PSB", "PBB")]
        if pr:
            reads = [r for r in reads if not (r and r[0] in ("PSB", "PBB"))]
            writes = list(writes) + pr
        need = {}
        for r in reads:
            w = self.last_w.get(r)
            if w is not None:
                need[w[0]] = max(need.get(w[0], 0), w[1])
        for r in writes:
            w = self.last_w.get(r)
            if w is not None:
                need[w[0]] = max(need.get(w[0], 0), w[1])
            for e2, c2 in self.readers.get(r, {}).items():
                need[e2] = max(need.get(e2, 0), c2)
        if eng == "pe":
            need.pop("pe", None)
        waits = []
        for tgt in sorted(need):
            c = need[tgt]
            if tgt in self.DTAGS:
                c = self.cnt[tgt]
            key = (eng, tgt)
            if c > self.mark.get(key, 0):
                self.mark[key] = c
                waits.append((tgt, c))
        acct = tag if tag is not None else eng
        self.cnt[acct] += self.STEP[acct]
        my = self.cnt[acct]
        sems = self.sems
        step = self.STEP[acct]

        def emit(eo, waits=tuple(waits), fn=fn, me=acct, st=step):
            for tgt, c in waits:
                eo.wait_ge(sems[tgt], c)
            fn(eo).then_inc(sems[me], st)

        self.streams[eng].append(emit)
        for r in writes:
            self.last_w[r] = (acct, my)
            self.readers[r] = {}
        for r in reads:
            self.readers.setdefault(r, {})[acct] = my
        return (acct, my)

    def final_wait(self, eng, tgt):
        c = self.cnt[tgt]
        sems = self.sems
        self.streams[eng].append(lambda eo, t=tgt, c=c: eo.wait_ge(sems[t], c))


def cdiv(a, b):
    return (a + b - 1) // b


def arnk(c0, c1):
    return [("ARN", b) for b in range(c0 // 512, cdiv(c1, 512))]


def scrk(c0, c1):
    return [("SCR", b) for b in range(c0 // 128, cdiv(c1, 128))]


def sck(c0, c1):
    return [("SC2", b) for b in range(c0 // 128, cdiv(c1, 128))]


def ctxk(c0, c1):
    return [("CTXH", b) for b in range(c0 // 128, cdiv(c1, 128))]


def expk(c0, c1):
    return [("EXPT", b) for b in range(c0 // 128, cdiv(c1, 128))]


def mm(kb, out_ap, lhsT_ap, rhs_ap, start, stop, reads, writes):
    kb.op("pe", lambda eo, o=out_ap, l=lhsT_ap, r=rhs_ap, s=start, p=stop:
          eo.matmul(o, l, r, start=s, stop=p, skip_group_check=True),
          reads=reads, writes=writes)


class Loader:

    def __init__(self, kb, STG):
        self.kb = kb
        self.STG = STG
        self.rr = 0

    def run(self, pieces, q="sp", ceng="gp"):
        for srcf, dstf, w, wkeys in pieces:
            slot = self.rr
            self.rr ^= 1
            stg = self.STG[:, slot * 512: slot * 512 + w]
            skey = ("STG", slot)
            self.kb.op(q, lambda eo, d=stg, s=srcf: eo.dma_start(d, s()),
                       reads=[skey], writes=[skey], tag="ds0" if slot == 0 else "ds1")
            if ceng == "gp":
                self.kb.op("gp", lambda eo, d=dstf, s=stg: eo.tensor_copy(d(), s),
                           reads=[skey], writes=list(wkeys))
            elif ceng == "act":
                self.kb.op(ceng, lambda eo, d=dstf, s=stg: eo.activation(d(), s, AF.Copy),
                           reads=[skey], writes=list(wkeys))
            else:
                self.kb.op(ceng, lambda eo, d=dstf, s=stg: eo.tensor_copy(d(), s),
                           reads=[skey], writes=list(wkeys))


class _unused:
    pass


def w_head_slice_pieces(din_t, dst, dst0, h, wkeys):
    out = []
    for g in range(2):
        def srcf(g=g):
            v = din_t.ap().rearrange("p (c j) -> p c j", c=ND)
            return v[:, g * 4:(g + 1) * 4, h * 128:(h + 1) * 128]

        def dstf(g=g):
            return dst[:, dst0 + g * 512: dst0 + (g + 1) * 512].rearrange(
                "p (c j) -> p c j", c=4)
        out.append((srcf, dstf, 512, wkeys))
    return out


def w_contig_pieces(din_t, col0, ncols, dst, dst0, wkeys):
    out = []
    for i in range(cdiv(ncols, 512)):
        w = min(512, ncols - i * 512)

        def srcf(i=i, w=w):
            return din_t.ap()[:, col0 + i * 512: col0 + i * 512 + w]

        def dstf(i=i, w=w):
            return dst[:, dst0 + i * 512: dst0 + i * 512 + w]
        out.append((srcf, dstf, w, wkeys))
    return out


RXN = [("XN", i) for i in range(NT)]
RSTG = [("STG", 0), ("STG", 1)]
RWA = [("WA", 0), ("WA", 1)]



def emit_ln(kb, B):
    X, XN, ARN, LNS, SCR = B["X"], B["XN"], B["ARN"], B["LNS"], B["SCR"]
    IDT, PBB = B["IDT"], B["PBB"]
    for i in range(NT):
        xt = X[:, i * D:(i + 1) * D]
        rX = ("X", i)
        kb.op("dve", lambda eo, o=SCR[:, 0:6], a=xt[:, 0:512]: eo.bn_stats(o, a),
              reads=[rX], writes=scrk(0, 128))
        kb.op("dve", lambda eo, o=SCR[:, 6:12], a=xt[:, 512:1024]: eo.bn_stats(o, a),
              reads=[rX], writes=scrk(0, 128))
        kb.op("dve", lambda eo, o=LNS[:, 2 * i:2 * i + 2], a=SCR[:, 0:12]: eo.bn_aggr(o, a),
              reads=scrk(0, 128), writes=[("LNS", i)])
    mv = LNS[:, 0:32].rearrange("p (i k) -> p i k", k=2)
    va = LNS[:, 32:48]
    rall = [("LNS", i) for i in range(NT)]
    kb.op("dve", lambda eo, o=va.rearrange("p (i k) -> p i k", k=1), a=mv[:, :, 1:2]:
          eo.tensor_scalar(o, a, EPS_LN, None, ALU.add), reads=rall, writes=[("LNS", "t")])
    kb.op("act", lambda eo, o=va, a=va: eo.activation(o, a, AF.Ln),
          reads=[("LNS", "t")], writes=[("LNS", "t")])
    kb.op("act", lambda eo, o=LNS[:, 48:64], a=va: eo.activation(o, a, AF.Exp, scale=-0.5),
          reads=[("LNS", "t")], writes=[("LNS", "a")])
    kb.op("dve", lambda eo, o=va.rearrange("p (i k) -> p i k", k=1), a=mv[:, :, 0:1]:
          eo.tensor_copy(o, a), reads=rall, writes=[("LNS", "t")])
    kb.op("dve", lambda eo, o=va, a=va, b=LNS[:, 48:64]: eo.tensor_tensor(o, a, b, ALU.mult),
          reads=[("LNS", "t"), ("LNS", "a")], writes=[("LNS", "t")])
    kb.op("dve", lambda eo, o=va, a=va: eo.tensor_scalar(o, a, -1.0, None, ALU.mult),
          reads=[("LNS", "t")], writes=[("LNS", "t")])
    for i in range(NT):
        slot = i % 2
        xst = ARN[:, slot * D:(slot + 1) * D]
        rst = arnk(slot * D, (slot + 1) * D)
        kb.op("act", lambda eo, o=xst, a=X[:, i * D:(i + 1) * D],
              s=LNS[:, 48 + i:49 + i], b=LNS[:, 32 + i:33 + i]:
              eo.activation(o, a, AF.Identity, bias=b, scale=s),
              reads=[("X", i), ("LNS", "a"), ("LNS", "t")], writes=rst)
        pb = PBB[i % 2]
        rpb = ("PBB", i % 2)
        for dc in range(ND):
            kb.op("pe", lambda eo, o=pb[:, dc * 128:(dc + 1) * 128],
                  a=xst[:, dc * 128:(dc + 1) * 128], idt=IDT[:, :]:
                  eo.transpose(o, a, idt),
                  reads=rst + [("IDT",)], writes=[rpb])

        def evac(eo, pbx=pb, i=i):
            src = pbx.ap().rearrange("p (c q) -> p c q", c=ND)
            dstv = B["XN"].ap().rearrange("p (c t) -> p c t", c=ND)[:, :, i * 128:(i + 1) * 128]
            return eo.tensor_copy(dstv, src)
        kb.op("dve", evac, reads=[rpb], writes=[("XN", i)])



def emit_ffn_gates(kb, B, ld, din):
    XN, GTM, GOTM, ARN, SCR, SC2 = (B["XN"], B["GTM"], B["GOTM"], B["ARN"],
                                    B["SCR"], B["SC2"])
    PSB, ROUTE = B["PSB"], B["ROUTE"]
    FSEL0 = 8192
    rFS = arnk(FSEL0, FSEL0 + 1024)
    ld.run(w_contig_pieces(din["fsel_t"], 0, 1024, ARN, FSEL0, rFS), ceng="gp")
    for i in range(NT):
        bank = PSB[i % 2]
        rb = ("PSB", i % 2)
        for dc in range(ND):
            mm(kb, bank[:, 0:128],
               XN[:, dc * T + i * 128: dc * T + (i + 1) * 128],
               ARN[:, FSEL0 + dc * 128:FSEL0 + (dc + 1) * 128], dc == 0, dc == ND - 1,
               reads=[RXN[i]] + rFS, writes=[rb])
        kb.op("act", lambda eo, o=SCR[:, 0:128], a=bank[:, 0:128]:
              eo.activation(o, a, AF.Sigmoid),
              reads=[rb], writes=scrk(0, 128))
        p3 = SCR[:, 0:128].rearrange("p (g e) -> p g e", e=8)
        m1 = SC2[:, 0:16]
        m13 = m1.rearrange("p (g o) -> p g o", o=1)
        m2 = SC2[:, 16:32]
        m23 = m2.rearrange("p (g o) -> p g o", o=1)
        e13 = SCR[:, 128:256].rearrange("p (g e) -> p g e", e=8)
        mk3 = SCR[:, 256:384].rearrange("p (g e) -> p g e", e=8)
        e23 = SCR[:, 384:512].rearrange("p (g e) -> p g e", e=8)
        f = SC2[:, 32:48]
        kb.op("dve", lambda eo, o=m13, a=p3: eo.reduce_max(o, a, AX.X),
              reads=scrk(0, 128), writes=sck(0, 128))
        kb.op("dve", lambda eo, o=e13, a=p3, b=m13.to_broadcast([128, 16, 8]):
              eo.tensor_tensor(o, a, b, ALU.is_equal),
              reads=scrk(0, 128) + sck(0, 128), writes=scrk(128, 256))
        kb.op("dve", lambda eo, o=mk3, a=e13, b=p3:
              eo.scalar_tensor_tensor(o, a, NEG_BIG, b, ALU.mult, ALU.add),
              reads=scrk(0, 256), writes=scrk(256, 384))
        kb.op("dve", lambda eo, o=m23, a=mk3: eo.reduce_max(o, a, AX.X),
              reads=scrk(256, 384), writes=sck(0, 128))
        kb.op("dve", lambda eo, o=f, a=m1, b=m2: eo.tensor_tensor(o, a, b, ALU.add),
              reads=sck(0, 128), writes=sck(0, 128))
        kb.op("dve", lambda eo, o=f, a=f: eo.tensor_scalar(o, a, 1e-9, None, ALU.max),
              reads=sck(0, 128), writes=sck(0, 128))
        kb.op("dve", lambda eo, o=f, a=f: eo.reciprocal(o, a),
              reads=sck(0, 128), writes=sck(0, 128))
        kb.op("dve", lambda eo, o=f, a=f, b=ROUTE[:, 0:1].to_broadcast([128, 16]):
              eo.tensor_tensor(o, a, b, ALU.mult),
              reads=sck(0, 128) + [("ROUTE",)], writes=sck(0, 128))
        kb.op("dve", lambda eo, o=m1, a=m1, b=f: eo.tensor_tensor(o, a, b, ALU.mult),
              reads=sck(0, 128), writes=sck(0, 128))
        kb.op("dve", lambda eo, o=e23, a=mk3, b=m23.to_broadcast([128, 16, 8]):
              eo.tensor_tensor(o, a, b, ALU.is_equal),
              reads=scrk(256, 384) + sck(0, 128), writes=scrk(384, 512))
        kb.op("dve", lambda eo, o=m2, a=m2, b=f: eo.tensor_tensor(o, a, b, ALU.mult),
              reads=sck(0, 128), writes=sck(0, 128))
        a1b = m13.to_broadcast([128, 16, 8])
        a2b = m23.to_broadcast([128, 16, 8])
        kb.op("dve", lambda eo, o=e13, a=e13, b=a1b: eo.tensor_tensor(o, a, b, ALU.mult),
              reads=scrk(128, 256) + sck(0, 128), writes=scrk(128, 256))
        kb.op("dve", lambda eo, o=e23, a=e23, b=a2b: eo.tensor_tensor(o, a, b, ALU.mult),
              reads=scrk(384, 512) + sck(0, 128), writes=scrk(384, 512))
        kb.op("dve", lambda eo, o=GTM[:, i * 64:(i + 1) * 64],
              a=SCR[:, 128:192], b=SCR[:, 384:448]:
              eo.tensor_tensor(o, a, b, ALU.add),
              reads=scrk(128, 512), writes=[("GTM", i)])
        kb.op("dve", lambda eo, o=GOTM[:, i * 64:(i + 1) * 64],
              a=SCR[:, 192:256], b=SCR[:, 448:512]:
              eo.tensor_tensor(o, a, b, ALU.add),
              reads=scrk(128, 512), writes=[("GOTM", i)])
        yield


def emit_ffn_head(kb, B, din, ld, h, last, pending, gates_gen=None):
    XN, GTM, GOTM, GFMH = B["XN"], B["GTM"], B["GOTM"], B["GFMH"]
    QH, KH, VH, WA, WB, QKW = B["QH"], B["KH"], B["VH"], B["WA"], B["WB"], B["QKW"]
    EXPT, CTXH, ARN, SCR, SC2, X = B["EXPT"], B["CTXH"], B["ARN"], B["SCR"], B["SC2"], B["X"]
    PSB, PBB, ONESB, SELE, IDT = B["PSB"], B["PBB"], B["ONESB"], B["SELE"], B["IDT"]

    def emit_gfmh(half):
        pb = PBB[half % 2]
        rpb = ("PBB", half % 2)
        for tl in range(8):
            i = half * 8 + tl
            kb.op("pe", lambda eo, o=pb[0:8, tl * 128:(tl + 1) * 128],
                  a=GOTM[:, i * 64 + h * 8: i * 64 + h * 8 + 8], idt=IDT[:, :]:
                  eo.transpose(o, a, idt),
                  reads=[("GOTM", half * 8 + tl), ("IDT",)], writes=[rpb])
        kb.op("dve", lambda eo, o=GFMH[0:8, 0:1024],
              a=pb[0:8, 0:1024]: eo.tensor_copy(o, a),
              reads=[rpb], writes=[("GFMH",)])
    rgf = [("GFMH",)]

    for wi, dstb, tag in ((0, QH, "QH"), (1, KH, "KH")):
        for g in range(4):
            bank = PSB[g % 2]
            rb = ("PSB", g % 2)
            for dc in range(ND):
                mm(kb, bank[:, :],
                   QKW[:, wi * 1024 + dc * 128: wi * 1024 + (dc + 1) * 128],
                   XN[:, dc * T + g * 512: dc * T + (g + 1) * 512],
                   dc == 0, dc == ND - 1,
                   reads=[RXN[4 * g], RXN[4 * g + 1], RXN[4 * g + 2], RXN[4 * g + 3],
                          ("QKW",)], writes=[rb])
            kb.op("dve", lambda eo, o=dstb[:, g * 512:(g + 1) * 512], a=bank[:, :]:
                  eo.tensor_copy(o, a),
                  reads=[rb], writes=[(tag, g)])
            if gates_gen is not None:
                next(gates_gen, None)
    rqh = [("QH", g) for g in range(4)]
    rkh = [("KH", g) for g in range(4)]

    if not last:
        ld.run(w_head_slice_pieces(din["fq_t"], QKW, 0, h + 1, [("QKW",)])
               + w_head_slice_pieces(din["fk_t"], QKW, 1024, h + 1, [("QKW",)]),
               q="sp", ceng="gp")

    for i in range(NT):
        if gates_gen is not None:
            next(gates_gen, None)
        b0 = 2 if i % 2 == 0 else 4
        banks = (PSB[b0], PSB[b0 + 1])
        rbs = (("PSB", b0), ("PSB", b0 + 1))
        for eg in range(2):
            for dc in range(ND):
                mm(kb, banks[eg][:, :],
                   XN[:, dc * T + i * 128: dc * T + (i + 1) * 128],
                   WA[:, dc * 1024 + eg * 512: dc * 1024 + (eg + 1) * 512],
                   dc == 0, dc == ND - 1,
                   reads=[RXN[i]] + RWA, writes=[rbs[eg]])
        vacc = SCR[:, 0:128]
        for e in range(8):
            bk, rb = (banks[0], rbs[0]) if e < 4 else (banks[1], rbs[1])
            gcol = GTM[:, i * 64 + h * 8 + e: i * 64 + h * 8 + e + 1]
            sl = bk[:, (e % 4) * 128:(e % 4 + 1) * 128]
            if e == 0:
                kb.op("dve", lambda eo, o=vacc, a=sl, g=gcol:
                      eo.tensor_scalar(o, a, g, None, ALU.mult),
                      reads=[rb, ("GTM", i)], writes=scrk(0, 128))
            else:
                kb.op("dve", lambda eo, o=vacc, a=sl, g=gcol, c=vacc:
                      eo.scalar_tensor_tensor(o, a, g, c, ALU.mult, ALU.add),
                      reads=[rb, ("GTM", i)] + scrk(0, 128), writes=scrk(0, 128))
        kb.op("dve", lambda eo, o=VH[:, i * 128:(i + 1) * 128], a=vacc:
              eo.tensor_copy(o, a),
              reads=scrk(0, 128), writes=[("VH", i)])
    rvh = [("VH", i) for i in range(NT)]

    if not last:
        pieces = []
        for dc in range(ND):
            pieces += w_contig_pieces(din["fv"], dc * H * D + (h + 1) * 1024, 1024,
                                      WA, dc * 1024, RWA)
        ld.run(pieces, q="sp", ceng="gp")

    ES = (EXPT[:, 0:512], EXPT[:, 512:1024], CTXH[:, 512:1024])
    rES = (expk(0, 512), expk(512, 1024), ctxk(512, 1024))
    for q4 in range(4):
        t0 = q4 * 512
        if q4 % 2 == 0:
            emit_gfmh(q4 // 2)
        sD, sC = ("PSB", 4), ("PSB", 5)

        def qk_exp(t2):
            a = t2 % 4
            s = t2 % 3
            mm(kb, PSB[a][:, :], KH[:, t2 * 128:(t2 + 1) * 128], QH[:, t0:t0 + 512],
               True, True, reads=[rkh[t2 // 4], rqh[q4]], writes=[("PSB", a)])
            kb.op("act", lambda eo, o=ES[s], i_=PSB[a][:, :]:
                  eo.activation(o, i_, AF.Exp, scale=SCALE_FFN),
                  reads=[("PSB", a)], writes=rES[s])

        def cs_pv(t2):
            s = t2 % 3
            es = ES[s]
            rE = rES[s]
            mm(kb, PSB[5][:, :], VH[:, t2 * 128:(t2 + 1) * 128], es,
               t2 == 0, t2 == NT - 1, reads=rE + [rvh[t2]], writes=[sC])
            mm(kb, PSB[4][:, :], ONESB[:, :], es, t2 == 0, t2 == NT - 1,
               reads=rE + [("ONESB",)], writes=[sD])

        qk_exp(0)
        qk_exp(1)
        for t2 in range(2, NT):
            qk_exp(t2)
            cs_pv(t2 - 2)
        cs_pv(NT - 2)
        cs_pv(NT - 1)
        kb.op("act", lambda eo, o=SC2[:, 0:512], a=PSB[4][:, :]:
              eo.activation(o, a, AF.Ln),
              reads=[sD], writes=sck(0, 512))
        kb.op("act", lambda eo, o=SC2[:, 0:512], a=SC2[:, 0:512]:
              eo.activation(o, a, AF.Exp, scale=-1.0),
              reads=sck(0, 512), writes=sck(0, 512))
        cs = CTXH[:, 0:512]
        rC = ctxk(0, 512)
        kb.op("dve", lambda eo, o=cs, a=PSB[5][:, :], b=SC2[:, 0:512]:
              eo.tensor_tensor(o, a, b, ALU.mult),
              reads=[sC] + sck(0, 512), writes=rC)
        aslot = (q4 % 2) * 4096
        jobs = pending[:]
        del pending[:]
        for e in range(8):
            if e < len(jobs):
                jobs[e]()
            pb = PSB[2 + e % 2]
            rpb = ("PSB", 2 + e % 2)
            gcol0 = t0 % 1024
            mm(kb, pb[:, :], SELE[0:8, e * 128:(e + 1) * 128],
               GFMH[0:8, gcol0:gcol0 + 512], True, True,
               reads=rgf + [("SELE",)], writes=[rpb])
            kb.op("dve", lambda eo, o=ARN[:, aslot + e * 512: aslot + (e + 1) * 512],
                  a=cs, b=pb[:, :]:
                  eo.tensor_tensor(o, a, b, ALU.mult),
                  reads=rC + [rpb], writes=arnk(aslot + e * 512, aslot + (e + 1) * 512))
        for e in range(8, len(jobs)):
            jobs[e]()
        rg = arnk(aslot, aslot + 4096)
        for tl in range(4):
            for dh in range(2):
                def opj(tl=tl, dh=dh, q4=q4, aslot=aslot, rg=rg):
                    ti = q4 * 4 + tl
                    bank = PSB[(tl * 2 + dh) % 2]
                    rb = ("PSB", (tl * 2 + dh) % 2)
                    for e in range(8):
                        mm(kb, bank[:, :],
                           ARN[:, aslot + e * 512 + tl * 128: aslot + e * 512 + (tl + 1) * 128],
                           WB[:, e * 1024 + dh * 512: e * 1024 + (dh + 1) * 512],
                           e == 0, e == E - 1, reads=rg + [("WB",)], writes=[rb])
                    kb.op("dve", lambda eo, o=X[:, ti * D + dh * 512: ti * D + (dh + 1) * 512],
                          a=bank[:, :], x=X[:, ti * D + dh * 512: ti * D + (dh + 1) * 512]:
                          eo.scalar_tensor_tensor(o, a, 0.5, x, ALU.mult, ALU.add),
                          reads=[rb, ("X", ti)], writes=[("X", ti)])
                pending.append(opj)

    for job in pending:
        job()
    del pending[:]

    if not last:
        ld.run(w_contig_pieces(din["fo"], (h + 1) * 8192, 8192, WB, 0, [("WB",)]),
               q="act", ceng="gp")


def emit_ffn(kb, B, din, ld):
    gates_gen = emit_ffn_gates(kb, B, ld, din)
    next(gates_gen, None)
    next(gates_gen, None)
    pieces = []
    for dc in range(ND):
        pieces += w_contig_pieces(din["fv"], dc * H * D, 1024, B["WA"], dc * 1024, RWA)
    ld.run(pieces, q="sp", ceng="gp")
    ld.run(w_head_slice_pieces(din["fq_t"], B["QKW"], 0, 0, [("QKW",)])
           + w_head_slice_pieces(din["fk_t"], B["QKW"], 1024, 0, [("QKW",)]),
           q="sp", ceng="gp")
    ld.run(w_contig_pieces(din["fo"], 0, 8192, B["WB"], 0, [("WB",)]), q="act", ceng="gp")
    pending = []
    for h in range(H):
        emit_ffn_head(kb, B, din, ld, h, last=(h == H - 1), pending=pending,
                      gates_gen=(gates_gen if h == 0 else None))
        if h == 0:
            for _ in gates_gen:
                pass



def emit_gla(kb, B, din, ld):
    XN, QH, VH, U1, WA, WB, QKW = (B["XN"], B["QH"], B["VH"], B["U1"], B["WA"],
                                   B["WB"], B["QKW"])
    ARN, G1, GOTM = B["ARN"], B["KH"], B["GOTM"]
    PSB, PBB, TRIU, IDT, ONESB, ONESF = (B["PSB"], B["PBB"], B["TRIU"], B["IDT"],
                                         B["ONESB"], B["ONESF"])
    X, SCR, SC2, EXPT, CTXH, GNW, STG = (B["X"], B["SCR"], B["SC2"], B["EXPT"],
                                         B["CTXH"], B["GNW"], B["STG"])

    def KT(s):   return SCR[:, s * 128:(s + 1) * 128]
    def rKT(s):  return scrk(s * 128, (s + 1) * 128)
    def DS(s):   return SCR[:, 256 + s * 128:256 + (s + 1) * 128]
    def rDS(s):  return scrk(256 + s * 128, 256 + (s + 1) * 128)
    def FP(s):   return SC2[:, s * 128:(s + 1) * 128]
    def rFP(s):  return sck(s * 128, (s + 1) * 128)
    def SS(s):   return SC2[:, 256 + s * 128:256 + (s + 1) * 128]
    def rSS(s):  return sck(256 + s * 128, 256 + (s + 1) * 128)
    def PTM(s):  return CTXH[:, s * 128:(s + 1) * 128]
    def rPTM(s): return ctxk(s * 128, (s + 1) * 128)
    def POF(s):  return CTXH[:, 256 + s * 128:256 + (s + 1) * 128]
    def rPOF(s): return ctxk(256 + s * 128, 256 + (s + 1) * 128)
    def KTT(s):  return CTXH[:, 512 + s * 128:512 + (s + 1) * 128]
    def rKTT(s): return ctxk(512 + s * 128, 512 + (s + 1) * 128)
    def SPB(s):  return CTXH[:, 768 + s * 128:768 + (s + 1) * 128]
    def rSPB(s): return ctxk(768 + s * 128, 768 + (s + 1) * 128)
    def KTQ(s):  return EXPT[:, s * 512:(s + 1) * 512]
    def rKTQ(s): return expk(s * 512, (s + 1) * 512)
    def CARRY(s): return B["LNS"][:, 2 * s:2 * s + 1]
    def rCARRY(s): return [("LNS", s)]
    def QHB(s):  return QH[:, s * 1024:(s + 1) * 1024]
    def KFB(s):  return U1[:, s * 1024:(s + 1) * 1024]
    def VHB(s):  return VH[:, s * 1024:(s + 1) * 1024]

    ld.run(w_contig_pieces(din["wg1_t"], 0, 1024, QKW, 0, [("QKW",)]), q="sp", ceng="gp")
    ld.run(w_contig_pieces(din["wg2_t"], 0, 1024, QKW, 1024, [("QKW",)]), q="sp", ceng="gp")

    def qkv_pieces(h):
        base = (h % 2) * 4096
        ks = [("WA", h % 2)]
        return (w_head_slice_pieces(din["wq_t"], WA, base, h, ks)
                + w_head_slice_pieces(din["wk_t"], WA, base + 1024, h, ks)
                + w_head_slice_pieces(din["wv_t"], WA, base + 2048, h, ks))
    ld.run(qkv_pieces(0), q="sp", ceng="gp")
    ld.run(qkv_pieces(1), q="sp", ceng="gp")

    for g in range(4):
        bank = PSB[g % 2]
        rb = ("PSB", g % 2)
        for dc in range(ND):
            mm(kb, bank[:, :], QKW[:, dc * 128:(dc + 1) * 128],
               XN[:, dc * T + g * 512: dc * T + (g + 1) * 512], dc == 0, dc == ND - 1,
               reads=[RXN[4 * g], RXN[4 * g + 1], RXN[4 * g + 2], RXN[4 * g + 3],
                      ("QKW",)], writes=[rb])
        kb.op("dve", lambda eo, o=G1[:, g * 512:(g + 1) * 512], a=bank[:, :]:
              eo.tensor_copy(o, a),
              reads=[rb], writes=[("KH", g)])
    rg1 = [("KH", g) for g in range(4)]

    def proj_q(s, h, qtr):
        base = (h % 2) * 4096
        rwa = ("WA", h % 2)
        slot = qtr % 2
        bq = ("QH2", s, slot)
        bk = ("U12", s, slot)
        bv = ("VH2", s, slot)
        t512 = qtr
        rx = [RXN[4 * t512 + j] for j in range(4)]
        bank = PSB[4 + s]
        rb = ("PSB", 4 + s)
        for dc in range(ND):
            mm(kb, bank[:, :],
               WA[:, base + dc * 128: base + (dc + 1) * 128],
               XN[:, dc * T + t512 * 512: dc * T + (t512 + 1) * 512],
               dc == 0, dc == ND - 1, reads=rx + [rwa], writes=[rb])
            yield
        kb.op("act", lambda eo, o=QHB(s)[:, slot * 512:(slot + 1) * 512], a=bank[:, :]:
              eo.activation(o, a, AF.Silu), reads=[rb], writes=[bq])
        yield
        for dc in range(ND):
            mm(kb, bank[:, :],
               WA[:, base + 1024 + dc * 128: base + 1024 + (dc + 1) * 128],
               XN[:, dc * T + t512 * 512: dc * T + (t512 + 1) * 512],
               dc == 0, dc == ND - 1, reads=rx + [rwa], writes=[rb])
            yield
        kb.op("dve", lambda eo, o=KFB(s)[:, slot * 512:(slot + 1) * 512], a=bank[:, :]:
              eo.tensor_copy(o, a), reads=[rb], writes=[bk])
        yield
        for dc in range(ND):
            mm(kb, bank[:, :],
               WA[:, base + 2048 + dc * 128: base + 2048 + (dc + 1) * 128],
               XN[:, dc * T + t512 * 512: dc * T + (t512 + 1) * 512],
               dc == 0, dc == ND - 1, reads=rx + [rwa], writes=[rb])
            yield
        vstg = GOTM[:, s * 512:(s + 1) * 512]
        rvs = ("GOTM", s)
        kb.op("dve", lambda eo, o=vstg, a=bank[:, :]: eo.tensor_copy(o, a),
              reads=[rb], writes=[rvs])
        yield
        pbv = PBB[s]
        rpbv = ("PBB", s)
        for tl in range(4):
            kb.op("pe", lambda eo, o=pbv[:, tl * 128:(tl + 1) * 128],
                  a=vstg[:, tl * 128:(tl + 1) * 128], idt=IDT[:, :]:
                  eo.transpose(o, a, idt),
                  reads=[rvs, ("IDT",)], writes=[rpbv])
            yield
        kb.op("dve", lambda eo, o=VHB(s)[:, slot * 512:(slot + 1) * 512],
              a=pbv[:, 0:512]: eo.tensor_copy(o, a),
              reads=[rpbv], writes=[bv])
        yield
    def fe_gen(s, h, qtr):
        slot = qtr % 2
        bq = ("QH2", s, slot)
        bk = ("U12", s, slot)
        kslot = KFB(s)[:, slot * 512:(slot + 1) * 512]
        kb.op("act", lambda eo, o=kslot, a=kslot: eo.activation(o, a, AF.Exp),
              reads=[bk], writes=[bk])
        yield
        kb.op("dve", lambda eo, o=KTQ(s), a=kslot: eo.tensor_copy(o, a),
              reads=[bk], writes=rKTQ(s))
        yield
        init = 0.0 if qtr == 0 else CARRY(s)
        rinit = [] if qtr == 0 else rCARRY(s)
        kb.op("dve", lambda eo, o=kslot, d0=ONESF[:, 0:1].to_broadcast([128, 512]),
              d1=kslot, i0=init:
              eo.tensor_tensor_scan(o, d0, d1, i0, ALU.mult, ALU.add),
              reads=[bk, ("ONESF",)] + rinit, writes=[bk])
        yield
        kb.op("dve", lambda eo, o=CARRY(s), a=kslot[:, 511:512]: eo.tensor_copy(o, a),
              reads=[bk], writes=rCARRY(s))
        yield
        kb.op("act", lambda eo, o=kslot, a=kslot: eo.activation(o, a, AF.Ln),
              reads=[bk], writes=[bk])
        yield
        kb.op("act", lambda eo, o=kslot, a=kslot: eo.activation(o, a, AF.Exp, scale=-1.0),
              reads=[bk], writes=[bk])
        yield
        qslot = QHB(s)[:, slot * 512:(slot + 1) * 512]
        kb.op("dve", lambda eo, o=qslot, a=qslot, b=kslot:
              eo.tensor_tensor(o, a, b, ALU.mult),
              reads=[bq, bk], writes=[bq])

    def fe_pair(hA, hB, qtr):
        import itertools as _it
        for a, b in _it.zip_longest(fe_gen(0, hA, qtr), fe_gen(1, hB, qtr)):
            pass

    def chunk(s, h, c):
        slot = (c // 4) % 2
        cl = slot * 512 + (c % 4) * 128
        qsl = QHB(s)[:, cl:cl + 128]
        vsl = VHB(s)[:, cl:cl + 128]
        ksl = KTQ(s)[:, (c % 4) * 128:(c % 4 + 1) * 128]
        bq, bv = ("QH2", s, slot), ("VH2", s, slot)
        bPT = ("PSB", 2 * s)
        bO = ("PSB", 2 * s + 1)
        even = (c % 2 == 0)
        last_w = c >= NCH - 2
        pt_dst = PSB[2 * s][:, 0:128] if even else PSB[2 * s][:, 256:384]
        mm(kb, pt_dst, ksl, qsl, even, True,
           reads=rKTQ(s) + [bq], writes=[bPT])
        yield
        if not even:
            ksl_prev = KTQ(s)[:, (c % 4 - 1) * 128:(c % 4) * 128]
            mm(kb, PSB[2 * s][:, 384:512], ksl_prev, qsl, False, True,
               reads=rKTQ(s) + [bq], writes=[bPT])
            yield
        kb.op("dve", lambda eo, o=PTM(s), a=pt_dst, b=TRIU[:, :]:
              eo.tensor_tensor(o, a, b, ALU.mult),
              reads=[bPT, ("TRIU",)], writes=rPTM(s))
        yield
        if not even:
            kb.op("act", lambda eo, o=POF(s), a=PSB[2 * s][:, 384:512]:
                  eo.activation(o, a, AF.Identity),
                  reads=[bPT], writes=rPOF(s))
            yield
        if not last_w:
            kb.op("pe", lambda eo, o=PBB[s][:, 512:640], a=ksl, idt=IDT[:, :]:
                  eo.transpose(o, a, idt),
                  reads=rKTQ(s) + [("IDT",)], writes=[("PBB", s)])
            yield
            kb.op("act", lambda eo, o=KTT(s), a=PBB[s][:, 512:640]:
                  eo.activation(o, a, AF.Identity),
                  reads=[("PBB", s)], writes=rKTT(s))
            yield
        oreg = PSB[2 * s + 1][:, (c % 2) * 128:(c % 2) * 128 + 128]
        mm(kb, oreg, vsl, PTM(s), even, False,
           reads=[bv] + rPTM(s), writes=[bO])
        yield
        if not even:
            vprev = VHB(s)[:, cl - 128:cl]
            mm(kb, oreg, vprev, POF(s), False, False,
               reads=[bv] + rPOF(s), writes=[bO])
            yield
        mm(kb, oreg, SPB(s), qsl, False, True,
           reads=rSPB(s) + [bq], writes=[bO])
        yield
        if not last_w:
            mm(kb, PSB[2 * s][:, 128:256], KTT(s), vsl, False, not even,
               reads=rKTT(s) + [bv], writes=[bPT])
            yield
            if not even:
                kb.op("dve", lambda eo, o=SPB(s), a=SPB(s), b=PSB[2 * s][:, 128:256]:
                      eo.tensor_tensor(o, a, b, ALU.add),
                      reads=rSPB(s) + [bPT], writes=rSPB(s))
                yield
        if not even:
            oc0 = h * T + (c - 1) * 128
            kb.op("act", lambda eo, o=ARN[:, oc0:oc0 + 256], a=PSB[2 * s + 1][:, 0:256]:
                  eo.activation(o, a, AF.Identity, scale=SCALE_GLA),
                  reads=[bO], writes=arnk(oc0, oc0 + 256))

    import itertools
    from collections import deque

    for hp in range(4):
        hA, hB = 2 * hp, 2 * hp + 1
        bg = deque()

        def drain_bg():
            while bg:
                g0 = bg[0]
                try:
                    next(g0)
                except StopIteration:
                    bg.popleft()

        def run_pair(c):
            ga, gb = chunk(0, hA, c), chunk(1, hB, c)
            while True:
                alive = False
                for g0 in (ga, gb):
                    try:
                        next(g0)
                        alive = True
                    except StopIteration:
                        pass
                if bg:
                    try:
                        next(bg[0])
                    except StopIteration:
                        bg.popleft()
                if not alive:
                    break

        ups = []
        if hp == 0:
            ups += [proj_q(0, hA, 0), proj_q(1, hB, 0)]
        ups += [proj_q(0, hA, 1), proj_q(1, hB, 1)]
        for g0 in ups:
            for _ in g0:
                pass
        fe_pair(hA, hB, 0)
        for s, h in ((0, hA), (1, hB)):
            kb.op("dve", lambda eo, o=SPB(s): eo.memset(o, 0.0), reads=[], writes=rSPB(s))
        for c in range(NCH):
            run_pair(c)
            if c == 3:
                fe_pair(hA, hB, 1)
                bg.append(proj_q(0, hA, 2))
                bg.append(proj_q(1, hB, 2))
            if c == 7:
                drain_bg()
                fe_pair(hA, hB, 2)
                bg.append(proj_q(0, hA, 3))
                bg.append(proj_q(1, hB, 3))
            if c == 11:
                drain_bg()
                fe_pair(hA, hB, 3)
                if hp + 1 < 4:
                    ld.run(qkv_pieces(2 * hp + 2), q="sp", ceng="gp")
                    ld.run(qkv_pieces(2 * hp + 3), q="sp", ceng="gp")
                    bg.append(proj_q(0, 2 * hp + 2, 0))
                    bg.append(proj_q(1, 2 * hp + 3, 0))
        drain_bg()
        if hp == 0:
            ld.run(w_contig_pieces(din["wo_t"], 0, 8192, WB, 0, [("WB",)]), q="act", ceng="gp")

    wo_jobs = []

    def wo_job(dc, g):
        bank = PSB[2 + (dc % 2)]
        rb = ("PSB", 2 + (dc % 2))
        for hh in range(ND):
            mm(kb, bank[:, :],
               WB[:, hh * 1024 + dc * 128: hh * 1024 + (dc + 1) * 128],
               ARN[:, hh * T + g * 512: hh * T + (g + 1) * 512],
               hh == 0, hh == ND - 1,
               reads=arnk(hh * T + g * 512, hh * T + (g + 1) * 512) + [("WB",)],
               writes=[rb])
        cslot = dc % 2
        kb.op("dve", lambda eo, o=CTXH[:, cslot * 512:(cslot + 1) * 512],
              a=bank[:, :]: eo.tensor_copy(o, a),
              reads=[rb], writes=ctxk(cslot * 512, (cslot + 1) * 512))
        pb = PBB[dc % 2]
        rpb = ("PBB", dc % 2)
        for tl in range(4):
            kb.op("pe", lambda eo, o=pb[:, tl * 128:(tl + 1) * 128],
                  a=CTXH[:, cslot * 512 + tl * 128: cslot * 512 + (tl + 1) * 128],
                  idt=IDT[:, :]: eo.transpose(o, a, idt),
                  reads=ctxk(cslot * 512, (cslot + 1) * 512) + [("IDT",)],
                  writes=[rpb])

        def xadd(eo, pb=pb, g=g, dc=dc):
            src_ = pb[:, 0:512].rearrange("p (q c) -> p q c", q=4)
            dstv = B["X"].ap().rearrange("p (i d) -> p i d", i=NT)[
                :, g * 4:(g + 1) * 4, dc * 128:(dc + 1) * 128]
            return eo.tensor_tensor(dstv, dstv, src_, ALU.add)
        kb.op("dve", xadd,
              reads=[rpb] + [("X", g * 4 + tl) for tl in range(4)],
              writes=[("X", g * 4 + tl) for tl in range(4)])

    for g in range(4):
        sq = PSB[4 + g % 2]
        rsq = ("PSB", 4 + g % 2)
        jobs = wo_jobs[:]
        del wo_jobs[:]
        ji = 0
        for hh in range(H):
            sqb = (EXPT[:, 0:512] if hh % 2 == 0 else EXPT[:, 512:1024])
            rsqb = (expk(0, 512) if hh % 2 == 0 else expk(512, 1024))
            kb.op("act", lambda eo, o=sqb, a=ARN[:, hh * T + g * 512: hh * T + (g + 1) * 512]:
                  eo.activation(o, a, AF.Square),
                  reads=arnk(hh * T + g * 512, hh * T + (g + 1) * 512),
                  writes=rsqb)
            mm(kb, sq[:, :], ONESB[:, :], sqb, hh == 0, hh == H - 1,
               reads=rsqb + [("ONESB",)], writes=[rsq])
            if ji < len(jobs):
                jobs[ji]()
                ji += 1
        inv = STG[:, 0:512]
        kb.op("dve", lambda eo, o=inv, a=sq[:, :]:
              eo.tensor_scalar(o, a, 1.0 / D, EPS_GLA, ALU.mult, ALU.add),
              reads=[rsq], writes=RSTG)
        kb.op("act", lambda eo, o=inv, a=inv: eo.activation(o, a, AF.Ln),
              reads=RSTG, writes=RSTG)
        kb.op("act", lambda eo, o=inv, a=inv: eo.activation(o, a, AF.Exp, scale=-0.5),
              reads=RSTG, writes=RSTG)
        for h in range(H):
            bank = PSB[h % 2]
            rb = ("PSB", h % 2)
            mm(kb, bank[:, :], QKW[:, 1024 + h * 128: 1024 + (h + 1) * 128],
               G1[:, g * 512:(g + 1) * 512], True, True,
               reads=[rg1[g], ("QKW",)], writes=[rb])
            gb = VH[:, (h % 2) * 512:(h % 2) * 512 + 512]
            kb.op("act", lambda eo, o=gb, a=bank[:, :]: eo.activation(o, a, AF.Silu),
                  reads=[rb], writes=[("VH2", 0, h % 2)])
            c0 = h * T + g * 512
            osl = ARN[:, c0:c0 + 512]
            ro = arnk(c0, c0 + 512)
            tmp = SC2[:, 0:512]
            kb.op("dve", lambda eo, o=tmp, a=osl, b=gb:
                  eo.tensor_tensor(o, a, b, ALU.mult),
                  reads=ro + [("VH2", 0, h % 2)] + sck(0, 512), writes=sck(0, 512))
            kb.op("dve", lambda eo, o=tmp, a=tmp, b=inv:
                  eo.tensor_tensor(o, a, b, ALU.mult),
                  reads=sck(0, 512) + RSTG, writes=sck(0, 512))
            kb.op("dve", lambda eo, o=osl, a=tmp, gw=GNW[:, h:h + 1]:
                  eo.tensor_scalar(o, a, gw, None, ALU.mult),
                  reads=sck(0, 512) + [("GNW",)], writes=ro)
            if ji < len(jobs):
                jobs[ji]()
                ji += 1
        for j in range(ji, len(jobs)):
            jobs[j]()
        for dc in range(ND):
            wo_jobs.append(lambda dc=dc, g=g: wo_job(dc, g))
    for j in wo_jobs:
        j()



def build_program():
    nc = bass.Bass("TRN2", target_bir_lowering=False)
    din = {}

    def inp(name, shape, dtype=F32):
        din[name] = nc.dram_tensor(name, list(shape), dtype, kind="ExternalInput")

    inp("x", [128, NT * D])
    inp("wq_t", [128, ND * D]); inp("wk_t", [128, ND * D]); inp("wv_t", [128, ND * D])
    inp("wg1_t", [128, ND * 128]); inp("wg2_t", [128, D]); inp("wo_t", [128, ND * D])
    inp("gnw", [128, ND])
    inp("fq_t", [128, ND * D]); inp("fk_t", [128, ND * D])
    inp("fv", [128, ND * H * D]); inp("fo", [128, H * E * D])
    inp("fsel_t", [128, ND * 128]); inp("route", [128, 1])
    inp("ident", [128, 128], BF16); inp("triu", [128, 128], BF16)
    inp("onesb", [128, 128], BF16); inp("onesf", [128, 128], F32)
    inp("sele", [8, 8 * 128], BF16)
    y = nc.dram_tensor("y", [128, NT * D], F32, kind="ExternalOutput")

    stk = contextlib.ExitStack()

    def sb(name, part, free, dt):
        return stk.enter_context(nc.sbuf_tensor(name, [part, free], dt))

    def psr(name, part, free, dt):
        return stk.enter_context(nc.psum_tensor(name, [part, free], dt))

    with stk:
        B = {}
        B["X"] = sb("X", 128, NT * D, F32)
        B["XN"] = sb("XN", 128, ND * T, BF16)
        B["ARN"] = sb("ARN", 128, 16384, BF16)
        B["WA"] = sb("WA", 128, 8192, BF16)
        B["WB"] = sb("WB", 128, 8192, BF16)
        B["QKW"] = sb("QKW", 128, 2048, BF16)
        B["STG"] = sb("STG", 128, 1024, F32)
        B["QH"] = sb("QH", 128, T, BF16)
        B["KH"] = sb("KH", 128, T, BF16)
        B["VH"] = sb("VH", 128, T, BF16)
        B["U1"] = sb("U1", 128, T, F32)
        B["EXPT"] = sb("EXPT", 128, 1024, BF16)
        B["CTXH"] = sb("CTXH", 128, 1024, BF16)
        B["GTM"] = sb("GTM", 128, NT * 64, F32)
        B["GOTM"] = sb("GOTM", 128, NT * 64, BF16)
        B["GFMH"] = sb("GFMH", 8, 1024, BF16)
        B["SCR"] = sb("SCR", 128, 512, F32)
        B["SC2"] = sb("SC2", 128, 512, F32)
        B["LNS"] = sb("LNS", 128, 64, F32)
        B["IDT"] = sb("IDT", 128, 128, BF16)
        B["TRIU"] = sb("TRIU", 128, 128, BF16)
        B["ONESB"] = sb("ONESB", 128, 128, BF16)
        B["ONESF"] = sb("ONESF", 128, 128, F32)
        B["SELE"] = sb("SELE", 8, 8 * 128, BF16)
        B["ROUTE"] = sb("ROUTE", 128, 1, F32)
        B["GNW"] = sb("GNW", 128, ND, F32)
        PSP = psr("psp", 128, 2048, F32)
        B["PSP"] = PSP
        B["PSB"] = [PSP[:, i * 512:(i + 1) * 512] for i in range(4)] + [
            psr("psb4", 128, 512, F32), psr("psb5", 128, 512, F32)]
        B["PBB"] = [psr(f"pbb{i}", 128, 1024, BF16) for i in range(2)]

        kb = KB(nc)
        with contextlib.ExitStack() as sst:
            for e in KB.ENGS + KB.DTAGS:
                kb.sems[e] = sst.enter_context(nc.semaphore("sem_" + e))
            ld = Loader(kb, B["STG"])

            for i in range(NT):
                kb.op("sp", lambda eo, d=B["X"][:, i * D:(i + 1) * D],
                      s=din["x"].ap()[:, i * D:(i + 1) * D]: eo.dma_start(d, s),
                      reads=[], writes=[("X", i)], tag="dx")
            for nm, bufn in (("ident", "IDT"), ("triu", "TRIU"), ("onesb", "ONESB"),
                             ("onesf", "ONESF"), ("sele", "SELE"),
                             ("route", "ROUTE"), ("gnw", "GNW")):
                kb.op("act", lambda eo, d=B[bufn][:, :], s=din[nm].ap(): eo.dma_start(d, s),
                      reads=[], writes=[(bufn,)], tag="dc")

            emit_ln(kb, B)
            emit_ffn(kb, B, din, ld)
            if N_STAGES >= 2:
                emit_ln(kb, B)
                emit_gla(kb, B, din, ld)
            if N_STAGES >= 3:
                emit_ln(kb, B)
                emit_ffn(kb, B, din, ld)

            for i in range(NT):
                kb.op("act", lambda eo, d=y.ap()[:, i * D:(i + 1) * D],
                      s=B["X"][:, i * D:(i + 1) * D]: eo.dma_start(d, s),
                      reads=[("X", i)], writes=[("Y", i)], tag="dy")
            kb.final_wait("act", "dy")

            with nc.Block() as blk:
                @blk.tensor
                def _(t):
                    for fn in kb.streams["pe"]:
                        fn(t)

                @blk.vector
                def _(v):
                    for fn in kb.streams["dve"]:
                        fn(v)

                @blk.scalar
                def _(s):
                    for fn in kb.streams["act"]:
                        fn(s)

                @blk.gpsimd
                def _(g):
                    for fn in kb.streams["gp"]:
                        fn(g)

                @blk.sync
                def _(s):
                    for fn in kb.streams["sp"]:
                        fn(s)
    return nc



def _prep_host(inputs):
    import ml_dtypes
    f = np.ascontiguousarray
    x = np.asarray(inputs["L0_attn_hidden_states"], dtype=np.float32)

    def fm_w(wt):
        w = np.asarray(wt, dtype=np.float32).T
        din_, dout = w.shape
        return f(w.reshape(din_ // 128, 128, dout).transpose(1, 0, 2).reshape(128, -1))

    sele = np.zeros((8, 8 * 128), dtype=np.float32)
    for e in range(8):
        sele[e, e * 128:(e + 1) * 128] = 1.0

    shared = {
        "wq_t": fm_w(inputs["L0_attn_Wq"]),
        "wk_t": fm_w(inputs["L0_attn_Wk"]),
        "wv_t": fm_w(inputs["L0_attn_Wv"]),
        "wg1_t": fm_w(inputs["L0_attn_Wg1"]),
        "wg2_t": f(np.asarray(inputs["L0_attn_Wg2"], dtype=np.float32).T),
        "wo_t": fm_w(inputs["L0_attn_Wo"]),
        "fq_t": fm_w(inputs["L0_ffn_q_w"]),
        "fk_t": fm_w(inputs["L0_ffn_k_w"]),
        "route": f(np.broadcast_to(
            np.asarray(inputs["L0_ffn_route_scale"], np.float32).reshape(1, 1), (128, 1))),
        "ident": np.eye(128, dtype=np.float32).astype(ml_dtypes.bfloat16),
        "triu": np.triu(np.ones((128, 128), np.float32)).astype(ml_dtypes.bfloat16),
        "onesb": np.ones((128, 128), np.float32).astype(ml_dtypes.bfloat16),
        "onesf": np.ones((128, 128), dtype=np.float32),
        "sele": sele.astype(ml_dtypes.bfloat16),
    }
    gn = np.asarray(inputs["L0_attn_g_norm_weight"], np.float32)
    shared["gnw"] = f(gn.reshape(ND, 128).T)
    vw = np.asarray(inputs["L0_ffn_v_w"], np.float32).reshape(H, E, D, 128)
    fv = vw.transpose(2, 0, 1, 3)
    shared["fv"] = f(fv.reshape(ND, 128, H * E * 128).transpose(1, 0, 2).reshape(128, -1))
    ow = np.asarray(inputs["L0_ffn_o_w"], np.float32).reshape(H, E, 128, D)
    shared["fo"] = f(ow.transpose(2, 0, 1, 3).reshape(128, -1))
    sv = np.asarray(inputs["L0_ffn_sel_v_w"], np.float32)
    so = np.asarray(inputs["L0_ffn_sel_o_w"], np.float32)
    sel = np.concatenate([sv, so], axis=0).T
    shared["fsel_t"] = f(sel.reshape(ND, 128, 128).transpose(1, 0, 2).reshape(128, -1))

    in_maps = []
    for b in range(8):
        m = dict(shared)
        m["x"] = f(x[b].reshape(NT, 128, D).transpose(1, 0, 2).reshape(128, -1))
        in_maps.append(m)
    return in_maps


def kernel(**inputs):
    in_maps = _prep_host(inputs)
    nc = build_program()
    res = run_bass_kernel_spmd(nc, in_maps, core_ids=list(range(8)))
    outs = []
    for r in res.results:
        yb = np.asarray(r["y"])
        outs.append(yb.reshape(128, NT, D).transpose(1, 0, 2).reshape(T, D))
    return np.stack(outs, axis=0)
